# version 04
# baseline (speedup 1.0000x reference)
import numpy as np
import ml_dtypes
from contextlib import ExitStack

import concourse.bass as bass
import concourse.mybir as mybir
from concourse.bass_utils import run_bass_kernel_spmd

F32 = mybir.dt.float32
BF16 = mybir.dt.bfloat16
I32 = mybir.dt.int32
FP8 = mybir.dt.float8e4
I8 = mybir.dt.int8
AF = mybir.ActivationFunctionType
ALU = mybir.AluOpType
AX = mybir.AxisListType

T = 2048
D = 1024
HP = 8
INNER = 4096
NTT = 16
NDC = 8
EPS_ATTN = 1e-6
EPS_LN = 1e-5

ARENA_BYTES = 207 * 1024


class Sched:

    ENGS = ("t", "v", "s", "g", "y")

    def __init__(self):
        self.steps = []
        self.tags = {}
        self.totals = {}

    def step(self, eng, fn, sem=None, inc=0, waits=(), tag=None, fence=False):
        assert eng in self.ENGS, eng
        for w in waits:
            assert w in self.tags, f"wait on undefined tag {w!r}"
        val = None
        if sem is not None and inc:
            self.totals[sem] = self.totals.get(sem, 0) + inc
            val = self.totals[sem]
            if tag is not None:
                assert tag not in self.tags, f"duplicate tag {tag!r}"
                self.tags[tag] = (sem, val)
        else:
            assert tag is None and not fence
        self.steps.append((eng, fn, sem, inc, val, tuple(waits), fence))

    def emit(self, eng_name, eng_obj, sems):
        wm = {}
        for (eng, fn, sem, inc, val, waits, fence) in self.steps:
            if eng != eng_name:
                continue
            for w in waits:
                wsem, wval = self.tags[w]
                if wm.get(wsem, 0) < wval:
                    eng_obj.wait_ge(sems[wsem], wval)
                    wm[wsem] = wval
            inst = fn(eng_obj)
            if sem is not None and inc:
                inst.then_inc(sems[sem], inc)
                if fence:
                    eng_obj.wait_ge(sems[sem], val)
                    wm[sem] = max(wm.get(sem, 0), val)


class Arena:

    def __init__(self, tensor):
        self.t = tensor
        self.free = [(0, ARENA_BYTES)]
        self.live = {}

    def alloc(self, name, cols, dt):
        dsz = {F32: 4, BF16: 2, I32: 4, FP8: 1}[dt]
        n = (cols * dsz + 63) // 64 * 64
        for i, (off, sz) in enumerate(self.free):
            if sz >= n:
                self.free[i] = (off + n, sz - n)
                if self.free[i][1] == 0:
                    self.free.pop(i)
                self.live[name] = (off, n)
                return self.t[:, off:off + cols * dsz].bitcast(dt)
        raise MemoryError(f"SBUF arena OOM for {name} ({n}B); free={self.free}")

    def release(self, *names):
        for name in names:
            off, n = self.live.pop(name)
            self.free.append((off, n))
        self.free.sort()
        merged = []
        for o, s in self.free:
            if merged and merged[-1][0] + merged[-1][1] == o:
                merged[-1] = [merged[-1][0], merged[-1][1] + s]
            else:
                merged.append([o, s])
        self.free = [tuple(x) for x in merged]


def build_nc():
    nc = bass.Bass("TRN2", target_bir_lowering=False, debug=False)

    x_tok = nc.dram_tensor("x_tok", [T, D], F32, kind="ExternalInput").ap()
    x_feat = nc.dram_tensor("x_feat", [D, T], F32, kind="ExternalInput").ap()
    wq_d = nc.dram_tensor("wq", [D, D], F32, kind="ExternalInput").ap()
    wk_d = nc.dram_tensor("wk", [D, D], F32, kind="ExternalInput").ap()
    wv_d = nc.dram_tensor("wv", [D, D], F32, kind="ExternalInput").ap()
    wo_d = nc.dram_tensor("wo", [D, D], F32, kind="ExternalInput").ap()
    wtg_d = nc.dram_tensor("wtg", [D, INNER], F32, kind="ExternalInput").ap()
    wtu_d = nc.dram_tensor("wtu", [D, INNER], F32, kind="ExternalInput").ap()
    wtd_d = nc.dram_tensor("wtd", [INNER, D], F32, kind="ExternalInput").ap()
    ident_d = nc.dram_tensor("ident", [128, 128], F32, kind="ExternalInput").ap()
    y_d = nc.dram_tensor("y", [T, D], F32, kind="ExternalOutput").ap()
    y1_d = nc.dram_tensor("y1scratch", [T, D], F32, kind="Internal").ap()

    S = Sched()

    with ExitStack() as ctx:
        arena_t = ctx.enter_context(nc.sbuf_tensor("arena", [128, ARENA_BYTES], I8))
        AR = Arena(arena_t)
        onesrow = ctx.enter_context(nc.sbuf_tensor("onesrow", [1, 128], F32))

        psA = [ctx.enter_context(nc.psum_tensor(f"psA{i}", [128, 512], F32)) for i in range(4)]
        psB = [ctx.enter_context(nc.psum_tensor(f"psB{i}", [128, 512], F32)) for i in range(3)]
        psT = ctx.enter_context(nc.psum_tensor("psT", [128, 1024], BF16))

        sems = {n: ctx.enter_context(nc.semaphore("sem_" + n))
                for n in ("ld", "ls", "ts", "vs", "ss", "gs")}

        sident = AR.alloc("ident", 128, BF16)
        ones_bf = AR.alloc("ones_bf", 2, BF16)
        ones_f = AR.alloc("ones_f", 2, F32)
        consts = AR.alloc("consts", 16, F32)
        nc.const_aps.register(F32, EPS_LN, consts[:, 0:1])
        stats = AR.alloc("stats", 256, F32)
        wsums = AR.alloc("wsums", 96, F32)
        rc = AR.alloc("rc", 16, F32)
        tmp = [AR.alloc(f"tmp{i}", 512, F32) for i in range(4)]

        MCLIP = {"g": 168, "u": 170, "d": 172}
        SW = {"g": 176, "u": 178, "d": 180}

        S.step("g", lambda g: g.dma_start(sident, ident_d), "ld", 16, tag="ident")
        S.step("g", lambda g: g.memset(ones_bf[:, 0:1], 1.0), "gs", 1, tag="ones_bf")
        S.step("g", lambda g: g.memset(ones_f[:, 0:1], 1.0), "gs", 1, tag="ones_f")
        S.step("g", lambda g: g.memset(onesrow[:], 1.0), "gs", 1, tag="ones_row")
        S.step("g", lambda g: g.memset(consts[:, 0:1], EPS_LN), "gs", 1, tag="c_eps")

        xT = AR.alloc("xT", NDC * T, BF16)
        wk_sb = AR.alloc("wk_sb", NDC * D, BF16)
        wv_sb = AR.alloc("wv_sb", NDC * D, BF16)
        wq_sb = AR.alloc("wq_sb", NDC * D, BF16)
        for dc in range(NDC):
            S.step("g", lambda g, dc=dc: g.dma_start(xT[:, dc * T:(dc + 1) * T],
                                                     x_feat[dc * 128:(dc + 1) * 128, :]),
                   "ld", 16, tag=f"xT{dc}")
        for nm, dst, src in (("wk", wk_sb, wk_d), ("wv", wv_sb, wv_d), ("wq", wq_sb, wq_d)):
            for dc in range(NDC):
                S.step("g", lambda g, dst=dst, src=src, dc=dc:
                       g.dma_start(dst[:, dc * D:(dc + 1) * D], src[dc * 128:(dc + 1) * 128, :]),
                       "ld", 16, tag=f"{nm}{dc}")

        wstage = [AR.alloc("wstage0", 2048, F32), AR.alloc("wstage1", 2048, F32)]
        stage_list = ([("g", ti, h) for ti in range(NDC) for h in range(2)] +
                      [("u", ti, h) for ti in range(NDC) for h in range(2)] +
                      [("d", sd, 0) for sd in range(16)])

        def stage_load_fn(buf, ent):
            mat = ent[0]
            if mat in ("g", "u"):
                _, ti, h = ent
                src = wtg_d if mat == "g" else wtu_d
                return lambda y: y.dma_start(wstage[buf],
                                             src[ti * 128:(ti + 1) * 128,
                                                 h * 2048:(h + 1) * 2048])
            _, sd, _ = ent
            return lambda y: y.dma_start(
                wstage[buf][:].rearrange("p (a b) -> p a b", a=2),
                wtd_d[:].rearrange("(a p) b -> p a b", p=128)[:, 2 * sd:2 * sd + 2, :])

        xtok_tl = [AR.alloc("xtok0", D, F32), AR.alloc("xtok1", D, F32)]
        dump = AR.alloc("dump", 1024, F32)

        def p1_consume(si):
            for q in range(2):
                S.step("s", lambda s, buf=si % 2, q=q, si=si:
                       s.activation(dump, wstage[buf][:, q * 1024:(q + 1) * 1024],
                                    AF.Abs, accum_out=wsums[:, 2 * si + q:2 * si + q + 1]),
                       "ss", 1, waits=[f"p1L{si}"], tag=(f"p1c{si}" if q == 1 else None))

        for si, ent in enumerate(stage_list):
            if si >= 2:
                p1_consume(si - 2)
            waits = [f"p1c{si - 2}"] if si >= 2 else ["wv7"]
            S.step("y", stage_load_fn(si % 2, ent), "ls", 16, waits=waits, tag=f"p1L{si}")
        p1_consume(46)
        p1_consume(47)

        wo_sb = AR.alloc("wo_sb", NDC * D, BF16)
        for dc in range(NDC):
            S.step("g", lambda g, dc=dc: g.dma_start(wo_sb[:, dc * D:(dc + 1) * D],
                                                     wo_d[dc * 128:(dc + 1) * 128, :]),
                   "ld", 16, tag=f"wo{dc}")

        bank_tag = {}

        def mm(bank_idx, psum_ap, lhsT, rhs, start, stop, waits=(), tag=None):
            S.step("t", lambda t, l=lhsT, r=rhs, o=psum_ap, a=start, z=stop:
                   t.matmul(o, l, r, start=a, stop=z, skip_group_check=True),
                   *(("ts", 1) if tag else (None, 0)), waits=waits, tag=tag)

        def phi_evac(key, psum_ap, dst_ap, pp, extra_waits=()):
            t0, t1 = tmp[2 * pp], tmp[2 * pp + 1]
            w = [key] + list(extra_waits)
            S.step("v", lambda v: v.tensor_scalar(t0, psum_ap, 0.0, None, ALU.min),
                   "vs", 1, waits=w, tag=key + "_t0")
            S.step("s", lambda s: s.activation(t1, t0, AF.Exp),
                   "ss", 1, waits=[key + "_t0"], tag=key + "_t1")
            S.step("v", lambda v: v.scalar_tensor_tensor(dst_ap, psum_ap, 0.0, t1,
                                                         ALU.max, ALU.add),
                   "vs", 1, waits=[key + "_t1"], tag=key + "_done")

        kf_tl = [AR.alloc("kf0", D, BF16), AR.alloc("kf1", D, BF16)]
        v_tl = [AR.alloc("v0", D, BF16), AR.alloc("v1", D, BF16)]
        qf = AR.alloc("qf", NDC * T, BF16)

        def proj_kv(tt):
            pb = tt % 2
            base_w = ([f"xT{dc}" for dc in range(NDC)] +
                      [f"wk{dc}" for dc in range(NDC)] + [f"wv{dc}" for dc in range(NDC)])
            for dc in range(NDC):
                for b, (wsb, oc) in enumerate(((wk_sb, 0), (wk_sb, 1), (wv_sb, 0), (wv_sb, 1))):
                    waits = []
                    if dc == 0:
                        waits = (base_w if tt == 0 else []) + \
                                ([bank_tag[b]] if b in bank_tag else []) + \
                                ([f"kvacc{tt - 2}"] if tt >= 2 and b == 0 else [])
                    mm(b, psA[b][:],
                       xT[:, dc * T + tt * 128: dc * T + tt * 128 + 128],
                       wsb[:, dc * D + oc * 512: dc * D + oc * 512 + 512],
                       start=(dc == 0), stop=(dc == NDC - 1), waits=waits,
                       tag=(f"p{tt}b{b}" if dc == NDC - 1 else None))
            for b in range(2):
                phi_evac(f"p{tt}b{b}", psA[b][:], kf_tl[pb][:, b * 512:(b + 1) * 512], b,
                         extra_waits=([f"kvacc{tt - 2}"] if tt >= 2 else []))
                bank_tag[b] = f"p{tt}b{b}_done"
            for b in (2, 3):
                S.step("s", lambda s, pb=pb, b=b:
                       s.copy(v_tl[pb][:, (b - 2) * 512:(b - 1) * 512], psA[b][:]),
                       "ss", 1,
                       waits=[f"p{tt}b{b}"] + ([f"kvacc{tt - 2}"] if tt >= 2 else []),
                       tag=f"p{tt}b{b}_done")
                bank_tag[b] = f"p{tt}b{b}_done"

        def kv_acc(tt):
            pb = tt % 2
            for hp in range(HP):
                pbank = psB[0] if hp < 4 else psB[1]
                off = (hp % 4) * 128
                w = [f"p{tt}b0_done", f"p{tt}b1_done", f"p{tt}b2_done", f"p{tt}b3_done"] \
                    if hp == 0 else []
                if tt == 0 and hp == 0:
                    w = w + ["ones_bf"]
                mm(None, pbank[:, off:off + 128],
                   kf_tl[pb][:, hp * 128:(hp + 1) * 128],
                   v_tl[pb][:, hp * 128:(hp + 1) * 128],
                   start=(tt == 0 and hp % 4 == 0), stop=False, waits=w)
                mm(None, psB[2][:, hp:hp + 1],
                   kf_tl[pb][:, hp * 128:(hp + 1) * 128],
                   ones_bf[:, 0:1],
                   start=(tt == 0 and hp == 0), stop=(tt == NTT - 1 and hp == HP - 1),
                   waits=(), tag=(f"kvacc{tt}" if hp == HP - 1 else None))

        for tt in range(NTT):
            proj_kv(tt)
            if tt >= 1:
                kv_acc(tt - 1)
        kv_acc(NTT - 1)
        AR.release("wk_sb", "wv_sb")

        for ot in range(NDC):
            for dc in range(NDC):
                for lc in range(4):
                    waits = []
                    if dc == 0:
                        waits = ([bank_tag[lc]] if lc in bank_tag else []) + \
                                ([f"wq{d2}" for d2 in range(NDC)] if ot == 0 and lc == 0 else [])
                    mm(lc, psA[lc][:],
                       wq_sb[:, dc * D + ot * 128: dc * D + ot * 128 + 128],
                       xT[:, dc * T + lc * 512: dc * T + lc * 512 + 512],
                       start=(dc == 0), stop=(dc == NDC - 1), waits=waits,
                       tag=(f"q{ot}l{lc}" if dc == NDC - 1 else None))
            for lc in range(4):
                phi_evac(f"q{ot}l{lc}", psA[lc][:],
                         qf[:, ot * T + lc * 512: ot * T + lc * 512 + 512], lc % 2)
                bank_tag[lc] = f"q{ot}l{lc}_done"

        def wsum_reduce(mi, name):
            c0 = 32 * mi
            rcol = 100 + 64 * mi
            mmw = [f"p1c{16 * mi + 15}", "ones_f", "ones_row"]
            mmw += [f"kvacc{NTT - 1}"]
            S.step("t", lambda t: t.matmul(psB[2][0:1, rcol:rcol + 32], ones_f[:, 0:1],
                                           wsums[:, c0:c0 + 32], start=False, stop=False,
                                           skip_group_check=True),
                   "ts", 1, waits=mmw, tag=f"wr{name}_mm")
            S.step("v", lambda v: v.tensor_reduce(stats[0:1, 160:161],
                                                  psB[2][0:1, rcol:rcol + 32],
                                                  AX.X, ALU.add),
                   "vs", 1, waits=[f"wr{name}_mm"], tag=f"wr{name}_red", fence=True)
            S.step("v", lambda v: v.tensor_scalar(stats[0:1, 161:162], stats[0:1, 160:161],
                                                  1.0 / (D * INNER), EPS_LN, ALU.mult, ALU.max),
                   "vs", 1, tag=f"wr{name}_mc", fence=True)
            S.step("t", lambda t: t.matmul(psB[2][0:128, rcol + 40:rcol + 41], onesrow[:],
                                           stats[0:1, 161:162], start=False, stop=False,
                                           skip_group_check=True),
                   "ts", 1, waits=[f"wr{name}_mc"], tag=f"wr{name}_bc")
            mc = MCLIP[name]
            sw = SW[name]
            S.step("v", lambda v: v.tensor_copy(stats[:, mc:mc + 1],
                                                psB[2][0:128, rcol + 40:rcol + 41]),
                   "vs", 1, waits=[f"wr{name}_bc"], tag=f"wr{name}_m0", fence=True)
            S.step("v", lambda v: v.tensor_copy(stats[:, mc + 1:mc + 2], stats[:, mc:mc + 1]),
                   "vs", 1, tag=f"wr{name}_m1", fence=True)
            S.step("v", lambda v: v.reciprocal(stats[:, sw:sw + 2], stats[:, mc:mc + 2]),
                   "vs", 1, tag=f"wr{name}_sw", fence=True)

        AR.release("xT", "wq_sb")
        wsum_reduce(0, "g")

        nd = AR.alloc("nd", HP * 136, BF16)
        for hp in range(HP):
            pbank = psB[0] if hp < 4 else psB[1]
            off = (hp % 4) * 128
            nb = hp * 136
            S.step("v", lambda v, nb=nb: v.memset(nd[:, nb:nb + 136], 0.0),
                   "vs", 1, waits=[f"kvacc{NTT-1}"] if hp == 0 else (), tag=f"nd{hp}_z")
            S.step("v", lambda v, nb=nb, pbank=pbank, off=off:
                   v.tensor_copy(nd[0:64, nb:nb + 64], pbank[0:64, off:off + 64]),
                   "vs", 1, tag=f"nd{hp}_a")
            S.step("v", lambda v, nb=nb, pbank=pbank, off=off:
                   v.tensor_copy(nd[64:128, nb + 64:nb + 128],
                                 pbank[64:128, off + 64:off + 128]),
                   "vs", 1, tag=f"nd{hp}_b")
            S.step("v", lambda v, nb=nb, hp=hp:
                   v.tensor_copy(nd[0:64, nb + 128:nb + 129], psB[2][0:64, hp:hp + 1]),
                   "vs", 1, tag=f"nd{hp}_c")
            S.step("v", lambda v, nb=nb, hp=hp:
                   v.tensor_copy(nd[64:128, nb + 129:nb + 130], psB[2][64:128, hp:hp + 1]),
                   "vs", 1, tag=f"nd{hp}_done")

        concat_tl = [AR.alloc("cc0", D, BF16), AR.alloc("cc1", D, BF16)]
        attnT_tl = [AR.alloc("at0", D, BF16), AR.alloc("at1", D, BF16)]
        xq_stage = AR.alloc("xq_stage", D, BF16)
        xqT = AR.alloc("xqT", NDC * T, BF16)
        i32st = AR.alloc("i32st", 2048, I32)

        def num_den(tt):
            pb = tt % 2
            for g4 in range(2):
                hps = [g4 * 4 + i for i in range(4)]
                for i, hp in enumerate(hps):
                    w = []
                    if tt == 0:
                        w += [f"nd{hp}_done", f"q{hp}l3_done"]
                    if i == 0:
                        w += [bank_tag[j] for j in range(4) if j in bank_tag]
                    mm(i, psA[i][:, 0:130],
                       qf[:, hp * T + tt * 128: hp * T + tt * 128 + 128],
                       nd[:, hp * 136: hp * 136 + 130],
                       start=True, stop=True, waits=w, tag=f"n{tt}h{hp}")
                for i, hp in enumerate(hps):
                    S.step("v", lambda v, i=i: v.tensor_scalar(rc[:, 4 * i:4 * i + 2],
                                                               psA[i][:, 128:130],
                                                               EPS_ATTN, None, ALU.add),
                           "vs", 1, waits=[f"n{tt}h{hp}"], tag=f"rca{tt}_{hp}")
                for i, hp in enumerate(hps):
                    S.step("v", lambda v, i=i: v.reciprocal(rc[:, 4 * i + 2:4 * i + 4],
                                                            rc[:, 4 * i:4 * i + 2]),
                           "vs", 1, tag=f"rcr{tt}_{hp}", fence=(i == 3))
                for i, hp in enumerate(hps):
                    cb = hp * 128
                    S.step("v", lambda v, i=i, cb=cb, pb=pb:
                           v.tensor_scalar(concat_tl[pb][:, cb:cb + 64], psA[i][:, 0:64],
                                           rc[:, 4 * i + 2:4 * i + 3], None, ALU.mult),
                           "vs", 1)
                    S.step("v", lambda v, i=i, cb=cb, pb=pb:
                           v.tensor_scalar(concat_tl[pb][:, cb + 64:cb + 128],
                                           psA[i][:, 64:128],
                                           rc[:, 4 * i + 3:4 * i + 4], None, ALU.mult),
                           "vs", 1, tag=f"cc{tt}_{hp}")
                    bank_tag[i] = f"cc{tt}_{hp}"

        def transpose_concat(tt):
            pb = tt % 2
            w = [f"cc{tt}_{hp}" for hp in range(HP)] + ["ident"]
            if 7 in bank_tag:
                w.append(bank_tag[7])
            for dq in range(NDC):
                S.step("t", lambda t, dq=dq, pb=pb:
                       t.transpose(psT[:, dq * 128:(dq + 1) * 128],
                                   concat_tl[pb][:, dq * 128:(dq + 1) * 128], sident),
                       *(("ts", 1) if dq == NDC - 1 else (None, 0)),
                       waits=(w if dq == 0 else ()),
                       tag=(f"tc{tt}" if dq == NDC - 1 else None))
            S.step("v", lambda v, pb=pb:
                   v.tensor_copy(attnT_tl[pb], psT[:, 0:1024]),
                   "vs", 1, waits=[f"tc{tt}"], tag=f"tc{tt}_e")
            bank_tag[7] = f"tc{tt}_e"

        def wo_resid_ln1(tt):
            pb = tt % 2
            for dq in range(NDC):
                for b in range(2):
                    w = []
                    if dq == 0:
                        w = [f"tc{tt}_e"] + [f"wo{d2}" for d2 in range(NDC) if tt == 0]
                        if tt == 0:
                            w += [f"nd{HP-1}_done"]
                        w += [bank_tag.get(f"wo{b}_bank")] if f"wo{b}_bank" in bank_tag else []
                    mm(None, psB[b][:],
                       attnT_tl[pb][:, dq * 128:(dq + 1) * 128],
                       wo_sb[:, dq * D + b * 512: dq * D + b * 512 + 512],
                       start=(dq == 0), stop=(dq == NDC - 1),
                       waits=[x for x in w if x],
                       tag=(f"wo{tt}b{b}" if dq == NDC - 1 else None))
            for b in range(2):
                S.step("v", lambda v, b=b, tt=tt:
                       v.scalar_tensor_tensor(xtok_tl[tt % 2][:, b * 512:(b + 1) * 512],
                                              psB[b][:], 1.0,
                                              xtok_tl[tt % 2][:, b * 512:(b + 1) * 512],
                                              ALU.mult, ALU.add,
                                              accum_out=stats[:, 2 * tt + b:2 * tt + b + 1]),
                       "vs", 1, waits=[f"wo{tt}b{b}", f"xtok{tt}"], tag=f"r1{tt}b{b}")
                bank_tag[f"wo{b}_bank"] = f"r1{tt}b{b}"
                S.step("s", lambda s, b=b, tt=tt:
                       s.activation(tmp[2], xtok_tl[tt % 2][:, b * 512:(b + 1) * 512],
                                    AF.Square,
                                    accum_out=stats[:, 32 + 2 * tt + b:32 + 2 * tt + b + 1]),
                       "ss", 1, waits=[f"r1{tt}b{b}"], tag=f"sq{tt}b{b}")
            c_sum, c_sq = 2 * tt, 32 + 2 * tt
            c_m, c_v, c_nm = 64 + tt, 80 + tt, 96 + tt
            S.step("v", lambda v: v.tensor_tensor(stats[:, c_m:c_m + 1],
                                                  stats[:, c_sum:c_sum + 1],
                                                  stats[:, c_sum + 1:c_sum + 2], ALU.add),
                   "vs", 1, waits=[f"r1{tt}b0", f"r1{tt}b1"], tag=f"ln1a{tt}", fence=True)
            S.step("v", lambda v: v.tensor_scalar(stats[:, c_m:c_m + 1],
                                                  stats[:, c_m:c_m + 1], 1.0 / D, None,
                                                  ALU.mult),
                   "vs", 1, tag=f"ln1b{tt}", fence=True)
            S.step("v", lambda v: v.tensor_tensor(stats[:, c_v:c_v + 1],
                                                  stats[:, c_sq:c_sq + 1],
                                                  stats[:, c_sq + 1:c_sq + 2], ALU.add),
                   "vs", 1, waits=[f"sq{tt}b0", f"sq{tt}b1"], tag=f"ln1c{tt}", fence=True)
            S.step("v", lambda v: v.tensor_scalar(stats[:, c_v:c_v + 1],
                                                  stats[:, c_v:c_v + 1], 1.0 / D, None,
                                                  ALU.mult),
                   "vs", 1, tag=f"ln1d{tt}", fence=True)
            S.step("v", lambda v: v.tensor_tensor(stats[:, 112:113], stats[:, c_m:c_m + 1],
                                                  stats[:, c_m:c_m + 1], ALU.mult),
                   "vs", 1, tag=f"ln1e{tt}", fence=True)
            S.step("v", lambda v: v.tensor_tensor(stats[:, c_v:c_v + 1],
                                                  stats[:, c_v:c_v + 1],
                                                  stats[:, 112:113], ALU.subtract),
                   "vs", 1, tag=f"ln1f{tt}", fence=True)
            S.step("s", lambda s: s.activation(stats[:, c_v:c_v + 1], stats[:, c_v:c_v + 1],
                                               AF.Ln, bias=EPS_LN),
                   "ss", 1, waits=[f"ln1f{tt}"], tag=f"ln1g{tt}", fence=True)
            S.step("s", lambda s: s.activation(stats[:, c_v:c_v + 1], stats[:, c_v:c_v + 1],
                                               AF.Exp, scale=-0.5),
                   "ss", 1, tag=f"ln1h{tt}", fence=True)
            S.step("v", lambda v: v.scalar_tensor_tensor(stats[:, c_nm:c_nm + 1],
                                                         stats[:, c_m:c_m + 1], -1.0,
                                                         stats[:, c_v:c_v + 1],
                                                         ALU.mult, ALU.mult),
                   "vs", 1, waits=[f"ln1h{tt}"], tag=f"ln1i{tt}", fence=True)
            S.step("v", lambda v, tt=tt:
                   v.tensor_scalar(xtok_tl[tt % 2], xtok_tl[tt % 2],
                                   stats[:, c_v:c_v + 1], stats[:, c_nm:c_nm + 1],
                                   ALU.mult, ALU.add),
                   "vs", 1, tag=f"y1{tt}")
            S.step("g", lambda g, tt=tt: g.dma_start(y1_d[tt * 128:(tt + 1) * 128, :],
                                                     xtok_tl[tt % 2]),
                   "ld", 16, waits=[f"y1{tt}"], tag=f"y1sp{tt}")
            c_mx, c_sx, c_rx = 112, 128 + tt, 144 + tt
            S.step("v", lambda v: v.tensor_reduce(stats[:, c_mx:c_mx + 1], xtok_tl[tt % 2],
                                                  AX.X, ALU.max, apply_absolute_value=True),
                   "vs", 1, waits=[f"y1{tt}"], tag=f"xqa{tt}", fence=True)
            S.step("v", lambda v: v.tensor_scalar(stats[:, c_mx:c_mx + 1],
                                                  stats[:, c_mx:c_mx + 1], EPS_LN, None,
                                                  ALU.max),
                   "vs", 1, tag=f"xqb{tt}", fence=True)
            S.step("v", lambda v: v.tensor_copy(stats[:, c_mx + 1:c_mx + 2],
                                                stats[:, c_mx:c_mx + 1]),
                   "vs", 1, tag=f"xqc{tt}", fence=True)
            S.step("v", lambda v: v.reciprocal(stats[:, 114:116], stats[:, c_mx:c_mx + 2]),
                   "vs", 1, tag=f"xqd{tt}", fence=True)
            S.step("v", lambda v: v.tensor_scalar(stats[:, c_sx:c_sx + 1], stats[:, 114:115],
                                                  127.0, None, ALU.mult),
                   "vs", 1, tag=f"xqe{tt}", fence=True)
            S.step("v", lambda v: v.tensor_copy(stats[:, 114:115], stats[:, c_sx:c_sx + 1]),
                   "vs", 1, tag=f"xqf{tt}", fence=True)
            S.step("v", lambda v: v.reciprocal(stats[:, 116:118], stats[:, 114:116]),
                   "vs", 1, tag=f"xqg{tt}", fence=True)
            S.step("v", lambda v: v.tensor_copy(stats[:, c_rx:c_rx + 1], stats[:, 116:117]),
                   "vs", 1, tag=f"xqh{tt}", fence=True)
            S.step("v", lambda v, tt=tt: v.tensor_scalar(i32st[:, 0:1024], xtok_tl[tt % 2],
                                                         stats[:, c_sx:c_sx + 1], None,
                                                         ALU.mult),
                   "vs", 1, tag=f"xqi{tt}")
            S.step("v", lambda v: v.tensor_scalar(xq_stage, i32st[:, 0:1024],
                                                  127.0, -128.0, ALU.min, ALU.max),
                   "vs", 1, waits=([f"tx{tt - 1}"] if tt >= 1 else []), tag=f"xqj{tt}")

        def transpose_xq(tt):
            w = [f"xqj{tt}", bank_tag[7]]
            for dc in range(NDC):
                S.step("t", lambda t, dc=dc:
                       t.transpose(psT[:, dc * 128:(dc + 1) * 128],
                                   xq_stage[:, dc * 128:(dc + 1) * 128], sident),
                       *(("ts", 1) if dc == NDC - 1 else (None, 0)),
                       waits=(w if dc == 0 else ()),
                       tag=(f"tx{tt}" if dc == NDC - 1 else None))
            S.step("v", lambda v, tt=tt:
                   v.tensor_copy(xqT[:].rearrange("p (dc t) -> p dc t", dc=NDC)
                                 [:, :, tt * 128:(tt + 1) * 128],
                                 psT[:, 0:1024].rearrange("p (dc c) -> p dc c", dc=NDC)),
                   "vs", 1, waits=[f"tx{tt}"], tag=f"tx{tt}_e")
            bank_tag[7] = f"tx{tt}_e"

        def xtok_load(tt):
            w = [f"y1sp{tt - 2}", f"xqi{tt - 2}"] if tt >= 2 else []
            S.step("y", lambda y, tt=tt:
                   y.dma_start(xtok_tl[tt % 2], x_tok[tt * 128:(tt + 1) * 128, :]),
                   "ls", 16, waits=w, tag=f"xtok{tt}")

        xtok_load(0)
        xtok_load(1)
        for tt in range(NTT):
            num_den(tt)
            transpose_concat(tt)
            if tt >= 1:
                transpose_xq(tt - 1)
            wo_resid_ln1(tt)
            if tt + 2 < NTT:
                xtok_load(tt + 2)
        transpose_xq(NTT - 1)

        wsum_reduce(1, "u")
        wsum_reduce(2, "d")
        AR.release("qf", "nd", "wo_sb", "dump", "cc0", "cc1", "at0", "at1",
                   "xq_stage", "kf0", "kf1", "v0", "v1", "xtok0", "xtok1",
                   "tmp3")

        tern_g = [AR.alloc(f"tern_g{i}", 4 * INNER, FP8) for i in range(2)]
        tern_u = [AR.alloc(f"tern_u{i}", 4 * INNER, FP8) for i in range(2)]
        tern_d = [AR.alloc(f"tern_d{i}", 16 * D, FP8) for i in range(2)]
        TERN = {"g": tern_g, "u": tern_u}

        def tern_gu_sl(mat, dc, off, n):
            seg = TERN[mat][dc // 4]
            base = (dc % 4) * INNER + off
            return seg[:, base:base + n]

        def tern_d_sl(it, off, n):
            seg = tern_d[it // 16]
            base = (it % 16) * D + off
            return seg[:, base:base + n]

        def p2_consume(si, ent):
            mat = ent[0]
            buf = si % 2
            S.step("v", lambda v, buf=buf, mat=mat:
                   v.tensor_scalar(i32st, wstage[buf], stats[:, SW[mat]:SW[mat] + 1],
                                   None, ALU.mult),
                   "vs", 1, waits=[f"p2L{si}", f"wr{mat}_sw"], tag=f"p2q{si}a")
            if mat in ("g", "u"):
                _, ti, h = ent
                dst = tern_gu_sl(mat, ti, h * 2048, 2048)
            else:
                _, sd, _ = ent
                dst = tern_d_sl(2 * sd, 0, 2048)
            S.step("v", lambda v, dst=dst:
                   v.tensor_scalar(dst, i32st, 1.0, -1.0, ALU.min, ALU.max),
                   "vs", 1, tag=f"p2q{si}b")

        for si, ent in enumerate(stage_list):
            if si >= 2:
                p2_consume(si - 2, stage_list[si - 2])
            S.step("y", stage_load_fn(si % 2, ent), "ls", 16,
                   waits=(["p1c47"] if si < 2 else [f"p2q{si - 2}b"]), tag=f"p2L{si}")
        p2_consume(46, stage_list[46])
        p2_consume(47, stage_list[47])

        y1c_tl = [AR.alloc("y1c0", D, F32)]
        gateh_seg = [AR.alloc(f"gateh{i}", 2048, F32) for i in range(2)]
        hq = AR.alloc("hq", INNER, BF16)
        hqT = AR.alloc("hqT", INNER, BF16)
        y2_tl = [AR.alloc("y2a", D, F32)]

        def gateh_sl(oc, n=512):
            return gateh_seg[oc // 4][:, (oc % 4) * 512:(oc % 4) * 512 + n]

        for c in range(NTT):
            cb = 0
            S.step("y", lambda y, c=c, cb=cb: y.dma_start(y1c_tl[cb],
                                                          y1_d[c * 128:(c + 1) * 128, :]),
                   "ls", 16,
                   waits=[f"y1sp{c}"] + ([f"r2{c - 1}b1"] if c >= 1 else []),
                   tag=f"y1r{c}")
            cgc = 204 + 2 * (c % 2)
            S.step("v", lambda v, c=c, cgc=cgc: v.tensor_tensor(stats[:, cgc:cgc + 1],
                                                       stats[:, 144 + c:145 + c],
                                                       stats[:, MCLIP['g']:MCLIP['g'] + 1],
                                                       ALU.mult),
                   "vs", 1,
                   waits=[f"wrg_sw", f"xqh{c}"] +
                         ([f"ge{c - 2}w1i3"] if c >= 2 else []),
                   tag=f"cg{c}", fence=True)
            S.step("v", lambda v, c=c: v.tensor_tensor(stats[:, 185:186],
                                                       stats[:, 144 + c:145 + c],
                                                       stats[:, MCLIP['u']:MCLIP['u'] + 1],
                                                       ALU.mult),
                   "vs", 1, waits=[f"wru_sw"], tag=f"cu{c}", fence=True)

            for wave in range(2):
                for dc in range(NDC):
                    for i in range(4):
                        oc = wave * 4 + i
                        w = []
                        if dc == 0:
                            w = [bank_tag[i]] if i in bank_tag else []
                            if c == 0 and wave == 0:
                                w += [f"tx{NTT-1}_e"]
                            if i == 0:
                                w += [f"p2q{2 * t2 + (1 if oc >= 4 else 0)}b"
                                      for t2 in range(NDC)] if c == 0 else []
                        mm(i, psA[i][:],
                           xqT[:, dc * T + c * 128: dc * T + c * 128 + 128],
                           tern_gu_sl("g", dc, oc * 512, 512),
                           start=(dc == 0), stop=(dc == NDC - 1),
                           waits=w, tag=(f"g{c}w{wave}i{i}" if dc == NDC - 1 else None))
                for i in range(4):
                    oc = wave * 4 + i
                    S.step("s", lambda s, i=i, oc=oc, c=c, cgc=cgc:
                           s.activation(gateh_sl(oc), psA[i][:],
                                        AF.Silu, scale=stats[:, cgc:cgc + 1]),
                           "ss", 1,
                           waits=[f"g{c}w{wave}i{i}", f"cg{c}"] +
                                 ([f"hqa{c - 1}_1"] if (c >= 1 and wave == 0 and i == 0)
                                  else []),
                           tag=f"ge{c}w{wave}i{i}")
                    bank_tag[i] = f"ge{c}w{wave}i{i}"
            for wave in range(2):
                for dc in range(NDC):
                    for i in range(4):
                        oc = wave * 4 + i
                        w = []
                        if dc == 0:
                            w = [bank_tag[i]] if i in bank_tag else []
                            if i == 0 and c == 0:
                                w += [f"p2q{16 + 2 * t2 + (1 if oc >= 4 else 0)}b"
                                      for t2 in range(NDC)]
                        mm(i, psA[i][:],
                           xqT[:, dc * T + c * 128: dc * T + c * 128 + 128],
                           tern_gu_sl("u", dc, oc * 512, 512),
                           start=(dc == 0), stop=(dc == NDC - 1),
                           waits=w, tag=(f"u{c}w{wave}i{i}" if dc == NDC - 1 else None))
                for i in range(4):
                    oc = wave * 4 + i
                    S.step("v", lambda v, i=i, oc=oc, c=c:
                           v.scalar_tensor_tensor(gateh_sl(oc),
                                                  psA[i][:], stats[:, 185:186],
                                                  gateh_sl(oc),
                                                  ALU.mult, ALU.mult),
                           "vs", 1,
                           waits=[f"u{c}w{wave}i{i}", f"ge{c}w{wave}i{i}", f"cu{c}"],
                           tag=f"h{c}w{wave}i{i}")
                    bank_tag[i] = f"h{c}w{wave}i{i}"

            hw = [f"h{c}w{wv}i{i}" for wv in range(2) for i in range(4)]
            S.step("v", lambda v: v.tensor_reduce(stats[:, 186:187], gateh_seg[0],
                                                  AX.X, ALU.max, apply_absolute_value=True),
                   "vs", 1, waits=hw, tag=f"hm0{c}", fence=True)
            S.step("v", lambda v: v.tensor_reduce(stats[:, 187:188], gateh_seg[1],
                                                  AX.X, ALU.max, apply_absolute_value=True),
                   "vs", 1, tag=f"hm1{c}", fence=True)
            S.step("v", lambda v: v.tensor_tensor(stats[:, 188:189], stats[:, 186:187],
                                                  stats[:, 187:188], ALU.max),
                   "vs", 1, tag=f"hm2{c}", fence=True)
            S.step("v", lambda v: v.tensor_scalar(stats[:, 188:189], stats[:, 188:189],
                                                  EPS_LN, None, ALU.max),
                   "vs", 1, tag=f"hm3{c}", fence=True)
            S.step("v", lambda v: v.tensor_copy(stats[:, 189:190], stats[:, 188:189]),
                   "vs", 1, tag=f"hm4{c}", fence=True)
            S.step("v", lambda v: v.reciprocal(stats[:, 190:192], stats[:, 188:190]),
                   "vs", 1, tag=f"hm5{c}", fence=True)
            S.step("v", lambda v: v.tensor_scalar(stats[:, 192:193], stats[:, 190:191],
                                                  127.0, None, ALU.mult),
                   "vs", 1, tag=f"hm6{c}", fence=True)
            S.step("v", lambda v: v.tensor_copy(stats[:, 190:191], stats[:, 192:193]),
                   "vs", 1, tag=f"hm7{c}", fence=True)
            S.step("v", lambda v: v.reciprocal(stats[:, 193:195], stats[:, 190:192]),
                   "vs", 1, tag=f"hm8{c}", fence=True)
            S.step("v", lambda v: v.tensor_tensor(stats[:, 195:196], stats[:, 193:194],
                                                  stats[:, MCLIP['d']:MCLIP['d'] + 1],
                                                  ALU.mult),
                   "vs", 1, waits=[f"wrd_sw"], tag=f"cd{c}", fence=True)
            for hh in range(2):
                S.step("v", lambda v, hh=hh:
                       v.tensor_scalar(i32st, gateh_seg[hh],
                                       stats[:, 192:193], None, ALU.mult),
                       "vs", 1, tag=f"hqa{c}_{hh}")
                S.step("v", lambda v, hh=hh:
                       v.tensor_scalar(hq[:, hh * 2048:(hh + 1) * 2048], i32st,
                                       127.0, -128.0, ALU.min, ALU.max),
                       "vs", 1, tag=f"hqb{c}_{hh}")

            for bt in range(4):
                w = [f"hqb{c}_{bt // 2}", bank_tag[7]]
                for j in range(8):
                    it = bt * 8 + j
                    S.step("t", lambda t, it=it, j=j:
                           t.transpose(psT[:, j * 128:(j + 1) * 128],
                                       hq[:, it * 128:(it + 1) * 128], sident),
                           *(("ts", 1) if j == 7 else (None, 0)),
                           waits=(w if j == 0 else ()),
                           tag=(f"th{c}_{bt}" if j == 7 else None))
                S.step("v", lambda v, bt=bt:
                       v.tensor_copy(hqT[:, bt * 1024:(bt + 1) * 1024], psT[:, 0:1024]),
                       "vs", 1, waits=[f"th{c}_{bt}"], tag=f"th{c}_{bt}_e")
                bank_tag[7] = f"th{c}_{bt}_e"

            for it in range(32):
                for b in range(2):
                    w = []
                    if it == 0:
                        w = [f"th{c}_0_e"]
                        w += [f"r2{c - 1}b{b}"] if c >= 1 else []
                        if c == 0:
                            w += [f"p2q{32 + sd}b" for sd in range(16)]
                            w += [f"r1{NTT-1}b{b}"]
                    elif it % 8 == 0:
                        w = [f"th{c}_{it // 8}_e"]
                    mm(None, psB[b][:],
                       hqT[:, it * 128:(it + 1) * 128],
                       tern_d_sl(it, b * 512, 512),
                       start=(it == 0), stop=(it == 31),
                       waits=w, tag=(f"d{c}b{b}" if it == 31 else None))
            for b in range(2):
                S.step("v", lambda v, b=b, c=c, cb=cb:
                       v.scalar_tensor_tensor(y2_tl[cb][:, b * 512:(b + 1) * 512],
                                              psB[b][:], stats[:, 195:196],
                                              y1c_tl[cb][:, b * 512:(b + 1) * 512],
                                              ALU.mult, ALU.add,
                                              accum_out=stats[:, 196 + b:197 + b]),
                       "vs", 1,
                       waits=[f"d{c}b{b}", f"y1r{c}", f"cd{c}"] +
                             ([f"yst{c - 1}"] if c >= 1 else []),
                       tag=f"r2{c}b{b}")
                S.step("s", lambda s, b=b, c=c, cb=cb:
                       s.activation(tmp[2], y2_tl[cb][:, b * 512:(b + 1) * 512],
                                    AF.Square, accum_out=stats[:, 198 + b:199 + b]),
                       "ss", 1, waits=[f"r2{c}b{b}"], tag=f"sq2{c}b{b}")
            S.step("v", lambda v: v.tensor_tensor(stats[:, 200:201], stats[:, 196:197],
                                                  stats[:, 197:198], ALU.add),
                   "vs", 1, waits=[f"r2{c}b0", f"r2{c}b1"], tag=f"l2a{c}", fence=True)
            S.step("v", lambda v: v.tensor_scalar(stats[:, 200:201], stats[:, 200:201],
                                                  1.0 / D, None, ALU.mult),
                   "vs", 1, tag=f"l2b{c}", fence=True)
            S.step("v", lambda v: v.tensor_tensor(stats[:, 201:202], stats[:, 198:199],
                                                  stats[:, 199:200], ALU.add),
                   "vs", 1, waits=[f"sq2{c}b0", f"sq2{c}b1"], tag=f"l2c{c}", fence=True)
            S.step("v", lambda v: v.tensor_scalar(stats[:, 201:202], stats[:, 201:202],
                                                  1.0 / D, None, ALU.mult),
                   "vs", 1, tag=f"l2d{c}", fence=True)
            S.step("v", lambda v: v.tensor_tensor(stats[:, 202:203], stats[:, 200:201],
                                                  stats[:, 200:201], ALU.mult),
                   "vs", 1, tag=f"l2e{c}", fence=True)
            S.step("v", lambda v: v.tensor_tensor(stats[:, 201:202], stats[:, 201:202],
                                                  stats[:, 202:203], ALU.subtract),
                   "vs", 1, tag=f"l2f{c}", fence=True)
            S.step("s", lambda s: s.activation(stats[:, 201:202], stats[:, 201:202],
                                               AF.Ln, bias=EPS_LN),
                   "ss", 1, waits=[f"l2f{c}"], tag=f"l2g{c}", fence=True)
            S.step("s", lambda s: s.activation(stats[:, 201:202], stats[:, 201:202],
                                               AF.Exp, scale=-0.5),
                   "ss", 1, tag=f"l2h{c}", fence=True)
            S.step("v", lambda v: v.scalar_tensor_tensor(stats[:, 203:204],
                                                         stats[:, 200:201], -1.0,
                                                         stats[:, 201:202],
                                                         ALU.mult, ALU.mult),
                   "vs", 1, waits=[f"l2h{c}"], tag=f"l2i{c}", fence=True)
            S.step("v", lambda v, cb=cb: v.tensor_scalar(y2_tl[cb], y2_tl[cb],
                                                         stats[:, 201:202],
                                                         stats[:, 203:204],
                                                         ALU.mult, ALU.add),
                   "vs", 1, tag=f"y2{c}")
            S.step("g", lambda g, c=c, cb=cb: g.dma_start(y_d[c * 128:(c + 1) * 128, :],
                                                          y2_tl[cb]),
                   "ld", 16, waits=[f"y2{c}"], tag=f"yst{c}")

        tot_ld = S.totals.get("ld", 0)
        tot_ls = S.totals.get("ls", 0)

        block = ctx.enter_context(nc.Block())

        @block.tensor
        def _(t):
            S.emit("t", t, sems)

        @block.vector
        def _(v):
            S.emit("v", v, sems)

        @block.scalar
        def _(s):
            S.emit("s", s, sems)

        @block.gpsimd
        def _(g):
            S.emit("g", g, sems)

        @block.sync
        def _(sy):
            S.emit("y", sy, sems)
            sy.wait_ge(sems["ld"], tot_ld)
            sy.wait_ge(sems["ls"], tot_ls)

    return nc


_NC_CACHE = None


def kernel(L0_attn_x, L0_attn_Wq, L0_attn_Wk, L0_attn_Wv, L0_attn_Wo,
           L0_ffn_w_gate, L0_ffn_w_up, L0_ffn_w_down):
    global _NC_CACHE
    x = np.ascontiguousarray(np.asarray(L0_attn_x, dtype=np.float32))
    wq = np.ascontiguousarray(np.asarray(L0_attn_Wq, dtype=np.float32))
    wk = np.ascontiguousarray(np.asarray(L0_attn_Wk, dtype=np.float32))
    wv = np.ascontiguousarray(np.asarray(L0_attn_Wv, dtype=np.float32))
    wo = np.ascontiguousarray(np.asarray(L0_attn_Wo, dtype=np.float32))
    wtg = np.ascontiguousarray(np.asarray(L0_ffn_w_gate, dtype=np.float32).T)
    wtu = np.ascontiguousarray(np.asarray(L0_ffn_w_up, dtype=np.float32).T)
    wtd = np.ascontiguousarray(np.asarray(L0_ffn_w_down, dtype=np.float32).T)
    ident = np.eye(128, dtype=np.float32)

    if _NC_CACHE is None:
        _NC_CACHE = build_nc()
    nc = _NC_CACHE

    in_maps = []
    for b in range(8):
        xb = np.ascontiguousarray(x[b])
        in_maps.append(dict(x_tok=xb, x_feat=np.ascontiguousarray(xb.T),
                            wq=wq, wk=wk, wv=wv, wo=wo,
                            wtg=wtg, wtu=wtu, wtd=wtd, ident=ident))
    res = run_bass_kernel_spmd(nc, in_maps, core_ids=list(range(8)))
    out = np.stack([np.asarray(res.results[b]["y"], dtype=np.float32) for b in range(8)])
    return out


# version 05
# speedup vs baseline: 1.0931x; 1.0931x over previous
import numpy as np
import ml_dtypes
from contextlib import ExitStack

import concourse.bass as bass
import concourse.mybir as mybir
from concourse.bass_utils import run_bass_kernel_spmd

F32 = mybir.dt.float32
BF16 = mybir.dt.bfloat16
I32 = mybir.dt.int32
FP8 = mybir.dt.float8e4
I8 = mybir.dt.int8
AF = mybir.ActivationFunctionType
ALU = mybir.AluOpType
AX = mybir.AxisListType

T = 2048
D = 1024
HP = 8
INNER = 4096
NTT = 16
NDC = 8
EPS_ATTN = 1e-6
EPS_LN = 1e-5

ARENA_BYTES = 207 * 1024


class Sched:

    ENGS = ("t", "v", "s", "g", "y")

    def __init__(self):
        self.steps = []
        self.tags = {}
        self.totals = {}

    def step(self, eng, fn, sem=None, inc=0, waits=(), tag=None, fence=False):
        assert eng in self.ENGS, eng
        for w in waits:
            assert w in self.tags, f"wait on undefined tag {w!r}"
        val = None
        if sem is not None and inc:
            self.totals[sem] = self.totals.get(sem, 0) + inc
            val = self.totals[sem]
            if tag is not None:
                assert tag not in self.tags, f"duplicate tag {tag!r}"
                self.tags[tag] = (sem, val)
        else:
            assert tag is None and not fence
        self.steps.append((eng, fn, sem, inc, val, tuple(waits), fence))

    def check_deadlock(self):
        import bisect
        from collections import defaultdict, deque
        achieve = {}
        for idx, (eng, fn, sem, inc, val, waits, fence) in enumerate(self.steps):
            if sem is not None and inc:
                achieve.setdefault(sem, []).append((val, idx))
        def achiever(sem, v):
            lst = achieve[sem]
            i = bisect.bisect_left([x[0] for x in lst], v)
            return lst[i][1] if i < len(lst) else None
        n = len(self.steps)
        edges = defaultdict(list)
        last = {}
        for idx, (eng, fn, sem, inc, val, waits, fence) in enumerate(self.steps):
            if eng in last:
                edges[last[eng]].append(idx)
            last[eng] = idx
            for w in waits:
                ws, wv = self.tags[w]
                a = achiever(ws, wv)
                if a is not None:
                    edges[a].append(idx)
        indeg = [0] * n
        for u, vs in edges.items():
            for v in vs:
                indeg[v] += 1
        q = deque(i for i in range(n) if indeg[i] == 0)
        seen = 0
        while q:
            u = q.popleft()
            seen += 1
            for v in edges[u]:
                indeg[v] -= 1
                if indeg[v] == 0:
                    q.append(v)
        assert seen == n, f"schedule deadlock: {n - seen} steps in a cycle"

    def emit(self, eng_name, eng_obj, sems):
        wm = {}
        for (eng, fn, sem, inc, val, waits, fence) in self.steps:
            if eng != eng_name:
                continue
            for w in waits:
                wsem, wval = self.tags[w]
                if wm.get(wsem, 0) < wval:
                    eng_obj.wait_ge(sems[wsem], wval)
                    wm[wsem] = wval
            inst = fn(eng_obj)
            if sem is not None and inc:
                inst.then_inc(sems[sem], inc)
                if fence:
                    eng_obj.wait_ge(sems[sem], val)
                    wm[sem] = max(wm.get(sem, 0), val)


class Arena:

    def __init__(self, tensor):
        self.t = tensor
        self.free = [(0, ARENA_BYTES)]
        self.live = {}

    def alloc(self, name, cols, dt):
        dsz = {F32: 4, BF16: 2, I32: 4, FP8: 1}[dt]
        n = (cols * dsz + 63) // 64 * 64
        for i, (off, sz) in enumerate(self.free):
            if sz >= n:
                self.free[i] = (off + n, sz - n)
                if self.free[i][1] == 0:
                    self.free.pop(i)
                self.live[name] = (off, n)
                return self.t[:, off:off + cols * dsz].bitcast(dt)
        raise MemoryError(f"SBUF arena OOM for {name} ({n}B); free={self.free}")

    def release(self, *names):
        for name in names:
            off, n = self.live.pop(name)
            self.free.append((off, n))
        self.free.sort()
        merged = []
        for o, s in self.free:
            if merged and merged[-1][0] + merged[-1][1] == o:
                merged[-1] = [merged[-1][0], merged[-1][1] + s]
            else:
                merged.append([o, s])
        self.free = [tuple(x) for x in merged]


def build_nc():
    nc = bass.Bass("TRN2", target_bir_lowering=False, debug=False)

    x_tok = nc.dram_tensor("x_tok", [T, D], F32, kind="ExternalInput").ap()
    x_feat = nc.dram_tensor("x_feat", [D, T], F32, kind="ExternalInput").ap()
    wq_d = nc.dram_tensor("wq", [D, D], F32, kind="ExternalInput").ap()
    wk_d = nc.dram_tensor("wk", [D, D], F32, kind="ExternalInput").ap()
    wv_d = nc.dram_tensor("wv", [D, D], F32, kind="ExternalInput").ap()
    wo_d = nc.dram_tensor("wo", [D, D], F32, kind="ExternalInput").ap()
    wtg_d = nc.dram_tensor("wtg", [D, INNER], F32, kind="ExternalInput").ap()
    wtu_d = nc.dram_tensor("wtu", [D, INNER], F32, kind="ExternalInput").ap()
    wtd_d = nc.dram_tensor("wtd", [INNER, D], F32, kind="ExternalInput").ap()
    ident_d = nc.dram_tensor("ident", [128, 128], F32, kind="ExternalInput").ap()
    y_d = nc.dram_tensor("y", [T, D], F32, kind="ExternalOutput").ap()
    y1_d = nc.dram_tensor("y1scratch", [T, D], F32, kind="Internal").ap()

    S = Sched()

    with ExitStack() as ctx:
        arena_t = ctx.enter_context(nc.sbuf_tensor("arena", [128, ARENA_BYTES], I8))
        AR = Arena(arena_t)
        onesrow = ctx.enter_context(nc.sbuf_tensor("onesrow", [1, 128], F32))

        psA = [ctx.enter_context(nc.psum_tensor(f"psA{i}", [128, 512], F32)) for i in range(4)]
        psB = [ctx.enter_context(nc.psum_tensor(f"psB{i}", [128, 512], F32)) for i in range(3)]
        psT = ctx.enter_context(nc.psum_tensor("psT", [128, 1024], BF16))

        sems = {n: ctx.enter_context(nc.semaphore("sem_" + n))
                for n in ("ld", "ls", "ts", "vs", "ss", "gs")}

        sident = AR.alloc("ident", 128, BF16)
        ones_bf = AR.alloc("ones_bf", 2, BF16)
        ones_f = AR.alloc("ones_f", 2, F32)
        consts = AR.alloc("consts", 16, F32)
        nc.const_aps.register(F32, EPS_LN, consts[:, 0:1])
        stats = AR.alloc("stats", 256, F32)
        wsums = AR.alloc("wsums", 96, F32)
        rc = AR.alloc("rc", 16, F32)
        tmp = [AR.alloc(f"tmp{i}", 512, F32) for i in range(4)]

        MCLIP = {"g": 168, "u": 170, "d": 172}
        SW = {"g": 176, "u": 178, "d": 180}

        S.step("g", lambda g: g.dma_start(sident, ident_d), "ld", 16, tag="ident")
        S.step("g", lambda g: g.memset(ones_bf[:, 0:1], 1.0), "gs", 1, tag="ones_bf")
        S.step("g", lambda g: g.memset(ones_f[:, 0:1], 1.0), "gs", 1, tag="ones_f")
        S.step("g", lambda g: g.memset(onesrow[:], 1.0), "gs", 1, tag="ones_row")
        S.step("g", lambda g: g.memset(consts[:, 0:1], EPS_LN), "gs", 1, tag="c_eps")

        xT = AR.alloc("xT", NDC * T, BF16)
        wk_sb = AR.alloc("wk_sb", NDC * D, BF16)
        wv_sb = AR.alloc("wv_sb", NDC * D, BF16)
        wq_sb = AR.alloc("wq_sb", NDC * D, BF16)
        for dc in range(NDC):
            S.step("g", lambda g, dc=dc: g.dma_start(xT[:, dc * T:(dc + 1) * T],
                                                     x_feat[dc * 128:(dc + 1) * 128, :]),
                   "ld", 16, tag=f"xT{dc}")
        for nm, dst, src in (("wk", wk_sb, wk_d), ("wv", wv_sb, wv_d), ("wq", wq_sb, wq_d)):
            for dc in range(NDC):
                S.step("g", lambda g, dst=dst, src=src, dc=dc:
                       g.dma_start(dst[:, dc * D:(dc + 1) * D], src[dc * 128:(dc + 1) * 128, :]),
                       "ld", 16, tag=f"{nm}{dc}")

        wstage = [AR.alloc("wstage0", 2048, F32), AR.alloc("wstage1", 2048, F32)]
        stage_list = ([("g", ti, h) for ti in range(NDC) for h in range(2)] +
                      [("u", ti, h) for ti in range(NDC) for h in range(2)] +
                      [("d", sd, 0) for sd in range(16)])

        def stage_load_fn(buf, ent):
            mat = ent[0]
            if mat in ("g", "u"):
                _, ti, h = ent
                src = wtg_d if mat == "g" else wtu_d
                return lambda y: y.dma_start(wstage[buf],
                                             src[ti * 128:(ti + 1) * 128,
                                                 h * 2048:(h + 1) * 2048])
            _, sd, _ = ent
            return lambda y: y.dma_start(
                wstage[buf][:].rearrange("p (a b) -> p a b", a=2),
                wtd_d[:].rearrange("(a p) b -> p a b", p=128)[:, 2 * sd:2 * sd + 2, :])

        xtok_tl = [AR.alloc("xtok0", D, F32), AR.alloc("xtok1", D, F32)]
        dump = AR.alloc("dump", 1024, F32)

        def p1_consume(si):
            for q in range(2):
                S.step("s", lambda s, buf=si % 2, q=q, si=si:
                       s.activation(dump, wstage[buf][:, q * 1024:(q + 1) * 1024],
                                    AF.Abs, accum_out=wsums[:, 2 * si + q:2 * si + q + 1]),
                       "ss", 1, waits=[f"p1L{si}"], tag=(f"p1c{si}" if q == 1 else None))

        for si, ent in enumerate(stage_list):
            if si >= 2:
                p1_consume(si - 2)
            waits = [f"p1c{si - 2}"] if si >= 2 else []
            S.step("y", stage_load_fn(si % 2, ent), "ls", 16, waits=waits, tag=f"p1L{si}")
        p1_consume(46)
        p1_consume(47)

        wo_sb = AR.alloc("wo_sb", NDC * D, BF16)
        for dc in range(NDC):
            S.step("g", lambda g, dc=dc: g.dma_start(wo_sb[:, dc * D:(dc + 1) * D],
                                                     wo_d[dc * 128:(dc + 1) * 128, :]),
                   "ld", 16, tag=f"wo{dc}")

        bank_tag = {}

        def mm(bank_idx, psum_ap, lhsT, rhs, start, stop, waits=(), tag=None):
            S.step("t", lambda t, l=lhsT, r=rhs, o=psum_ap, a=start, z=stop:
                   t.matmul(o, l, r, start=a, stop=z, skip_group_check=True),
                   *(("ts", 1) if tag else (None, 0)), waits=waits, tag=tag)

        def phi_evac(key, psum_ap, dst_ap, pp, extra_waits=()):
            t0, t1 = tmp[2 * pp], tmp[2 * pp + 1]
            w = [key] + list(extra_waits)
            S.step("v", lambda v: v.tensor_scalar(t0, psum_ap, 0.0, None, ALU.min),
                   "vs", 1, waits=w, tag=key + "_t0")
            S.step("s", lambda s: s.activation(t1, t0, AF.Exp),
                   "ss", 1, waits=[key + "_t0"], tag=key + "_t1")
            S.step("v", lambda v: v.scalar_tensor_tensor(dst_ap, psum_ap, 0.0, t1,
                                                         ALU.max, ALU.add),
                   "vs", 1, waits=[key + "_t1"], tag=key + "_done")

        kf_tl = [AR.alloc("kf0", D, BF16), AR.alloc("kf1", D, BF16)]
        v_tl = [AR.alloc("v0", D, BF16), AR.alloc("v1", D, BF16)]
        qf = AR.alloc("qf", NDC * T, BF16)

        def proj_kv(tt):
            pb = tt % 2
            base_w = ([f"xT{dc}" for dc in range(NDC)] +
                      [f"wk{dc}" for dc in range(NDC)] + [f"wv{dc}" for dc in range(NDC)])
            for dc in range(NDC):
                for b, (wsb, oc) in enumerate(((wk_sb, 0), (wk_sb, 1), (wv_sb, 0), (wv_sb, 1))):
                    waits = []
                    if dc == 0:
                        waits = (base_w if tt == 0 else []) + \
                                ([bank_tag[b]] if b in bank_tag else []) + \
                                ([f"kvacc{tt - 2}"] if tt >= 2 and b == 0 else [])
                    mm(b, psA[b][:],
                       xT[:, dc * T + tt * 128: dc * T + tt * 128 + 128],
                       wsb[:, dc * D + oc * 512: dc * D + oc * 512 + 512],
                       start=(dc == 0), stop=(dc == NDC - 1), waits=waits,
                       tag=(f"p{tt}b{b}" if dc == NDC - 1 else None))
            for b in range(2):
                phi_evac(f"p{tt}b{b}", psA[b][:], kf_tl[pb][:, b * 512:(b + 1) * 512], b,
                         extra_waits=([f"kvacc{tt - 2}"] if tt >= 2 else []))
                bank_tag[b] = f"p{tt}b{b}_done"
            for b in (2, 3):
                S.step("s", lambda s, pb=pb, b=b:
                       s.copy(v_tl[pb][:, (b - 2) * 512:(b - 1) * 512], psA[b][:]),
                       "ss", 1,
                       waits=[f"p{tt}b{b}"] + ([f"kvacc{tt - 2}"] if tt >= 2 else []),
                       tag=f"p{tt}b{b}_done")
                bank_tag[b] = f"p{tt}b{b}_done"

        def kv_acc(tt):
            pb = tt % 2
            for hp in range(HP):
                pbank = psB[0] if hp < 4 else psB[1]
                off = (hp % 4) * 128
                w = [f"p{tt}b0_done", f"p{tt}b1_done", f"p{tt}b2_done", f"p{tt}b3_done"] \
                    if hp == 0 else []
                if tt == 0 and hp == 0:
                    w = w + ["ones_bf"]
                mm(None, pbank[:, off:off + 128],
                   kf_tl[pb][:, hp * 128:(hp + 1) * 128],
                   v_tl[pb][:, hp * 128:(hp + 1) * 128],
                   start=(tt == 0 and hp % 4 == 0), stop=False, waits=w)
                mm(None, psB[2][:, hp:hp + 1],
                   kf_tl[pb][:, hp * 128:(hp + 1) * 128],
                   ones_bf[:, 0:1],
                   start=(tt == 0 and hp == 0), stop=(tt == NTT - 1 and hp == HP - 1),
                   waits=(), tag=(f"kvacc{tt}" if hp == HP - 1 else None))

        for tt in range(NTT):
            proj_kv(tt)
            if tt >= 1:
                kv_acc(tt - 1)
        kv_acc(NTT - 1)
        AR.release("wk_sb", "wv_sb")

        for ot in range(NDC):
            for dc in range(NDC):
                for lc in range(4):
                    waits = []
                    if dc == 0:
                        waits = ([bank_tag[lc]] if lc in bank_tag else []) + \
                                ([f"wq{d2}" for d2 in range(NDC)] if ot == 0 and lc == 0 else [])
                    mm(lc, psA[lc][:],
                       wq_sb[:, dc * D + ot * 128: dc * D + ot * 128 + 128],
                       xT[:, dc * T + lc * 512: dc * T + lc * 512 + 512],
                       start=(dc == 0), stop=(dc == NDC - 1), waits=waits,
                       tag=(f"q{ot}l{lc}" if dc == NDC - 1 else None))
            for lc in range(4):
                phi_evac(f"q{ot}l{lc}", psA[lc][:],
                         qf[:, ot * T + lc * 512: ot * T + lc * 512 + 512], lc % 2)
                bank_tag[lc] = f"q{ot}l{lc}_done"

        def wsum_reduce(mi, name):
            c0 = 32 * mi
            rcol = 100 + 64 * mi
            mmw = [f"p1c{16 * mi + 15}", "ones_f", "ones_row"]
            mmw += [f"kvacc{NTT - 1}"]
            S.step("t", lambda t: t.matmul(psB[2][0:1, rcol:rcol + 32], ones_f[:, 0:1],
                                           wsums[:, c0:c0 + 32], start=False, stop=False,
                                           skip_group_check=True),
                   "ts", 1, waits=mmw, tag=f"wr{name}_mm")
            S.step("v", lambda v: v.tensor_reduce(stats[0:1, 160:161],
                                                  psB[2][0:1, rcol:rcol + 32],
                                                  AX.X, ALU.add),
                   "vs", 1, waits=[f"wr{name}_mm"], tag=f"wr{name}_red", fence=True)
            S.step("v", lambda v: v.tensor_scalar(stats[0:1, 161:162], stats[0:1, 160:161],
                                                  1.0 / (D * INNER), EPS_LN, ALU.mult, ALU.max),
                   "vs", 1, tag=f"wr{name}_mc", fence=True)
            S.step("t", lambda t: t.matmul(psB[2][0:128, rcol + 40:rcol + 41], onesrow[:],
                                           stats[0:1, 161:162], start=False, stop=False,
                                           skip_group_check=True),
                   "ts", 1, waits=[f"wr{name}_mc"], tag=f"wr{name}_bc")
            mc = MCLIP[name]
            sw = SW[name]
            S.step("v", lambda v: v.tensor_copy(stats[:, mc:mc + 1],
                                                psB[2][0:128, rcol + 40:rcol + 41]),
                   "vs", 1, waits=[f"wr{name}_bc"], tag=f"wr{name}_m0", fence=True)
            S.step("v", lambda v: v.tensor_copy(stats[:, mc + 1:mc + 2], stats[:, mc:mc + 1]),
                   "vs", 1, tag=f"wr{name}_m1", fence=True)
            S.step("v", lambda v: v.reciprocal(stats[:, sw:sw + 2], stats[:, mc:mc + 2]),
                   "vs", 1, tag=f"wr{name}_sw", fence=True)

        AR.release("xT", "wq_sb")
        wsum_reduce(0, "g")

        nd = AR.alloc("nd", HP * 136, BF16)
        for hp in range(HP):
            pbank = psB[0] if hp < 4 else psB[1]
            off = (hp % 4) * 128
            nb = hp * 136
            S.step("v", lambda v, nb=nb: v.memset(nd[:, nb:nb + 136], 0.0),
                   "vs", 1, waits=[f"kvacc{NTT-1}"] if hp == 0 else (), tag=f"nd{hp}_z")
            S.step("v", lambda v, nb=nb, pbank=pbank, off=off:
                   v.tensor_copy(nd[0:64, nb:nb + 64], pbank[0:64, off:off + 64]),
                   "vs", 1, tag=f"nd{hp}_a")
            S.step("v", lambda v, nb=nb, pbank=pbank, off=off:
                   v.tensor_copy(nd[64:128, nb + 64:nb + 128],
                                 pbank[64:128, off + 64:off + 128]),
                   "vs", 1, tag=f"nd{hp}_b")
            S.step("v", lambda v, nb=nb, hp=hp:
                   v.tensor_copy(nd[0:64, nb + 128:nb + 129], psB[2][0:64, hp:hp + 1]),
                   "vs", 1, tag=f"nd{hp}_c")
            S.step("v", lambda v, nb=nb, hp=hp:
                   v.tensor_copy(nd[64:128, nb + 129:nb + 130], psB[2][64:128, hp:hp + 1]),
                   "vs", 1, tag=f"nd{hp}_done")

        concat_tl = [AR.alloc("cc0", D, BF16), AR.alloc("cc1", D, BF16)]
        attnT_tl = [AR.alloc("at0", D, BF16), AR.alloc("at1", D, BF16)]
        xq_stage = AR.alloc("xq_stage", D, BF16)
        xqT = AR.alloc("xqT", NDC * T, BF16)
        i32st = AR.alloc("i32st", 2048, I32)

        def num_den(tt):
            pb = tt % 2
            for g4 in range(2):
                hps = [g4 * 4 + i for i in range(4)]
                for i, hp in enumerate(hps):
                    w = []
                    if tt == 0:
                        w += [f"nd{hp}_done", f"q{hp}l3_done"]
                    if i == 0:
                        w += [bank_tag[j] for j in range(4) if j in bank_tag]
                    mm(i, psA[i][:, 0:130],
                       qf[:, hp * T + tt * 128: hp * T + tt * 128 + 128],
                       nd[:, hp * 136: hp * 136 + 130],
                       start=True, stop=True, waits=w, tag=f"n{tt}h{hp}")
                for i, hp in enumerate(hps):
                    S.step("v", lambda v, i=i: v.tensor_scalar(rc[:, 4 * i:4 * i + 2],
                                                               psA[i][:, 128:130],
                                                               EPS_ATTN, None, ALU.add),
                           "vs", 1, waits=[f"n{tt}h{hp}"], tag=f"rca{tt}_{hp}")
                for i, hp in enumerate(hps):
                    S.step("v", lambda v, i=i: v.reciprocal(rc[:, 4 * i + 2:4 * i + 4],
                                                            rc[:, 4 * i:4 * i + 2]),
                           "vs", 1, tag=f"rcr{tt}_{hp}", fence=(i == 3))
                for i, hp in enumerate(hps):
                    cb = hp * 128
                    S.step("v", lambda v, i=i, cb=cb, pb=pb:
                           v.tensor_scalar(concat_tl[pb][:, cb:cb + 64], psA[i][:, 0:64],
                                           rc[:, 4 * i + 2:4 * i + 3], None, ALU.mult),
                           "vs", 1)
                    S.step("v", lambda v, i=i, cb=cb, pb=pb:
                           v.tensor_scalar(concat_tl[pb][:, cb + 64:cb + 128],
                                           psA[i][:, 64:128],
                                           rc[:, 4 * i + 3:4 * i + 4], None, ALU.mult),
                           "vs", 1, tag=f"cc{tt}_{hp}")
                    bank_tag[i] = f"cc{tt}_{hp}"

        def transpose_concat(tt):
            pb = tt % 2
            w = [f"cc{tt}_{hp}" for hp in range(HP)] + ["ident"]
            if 7 in bank_tag:
                w.append(bank_tag[7])
            for dq in range(NDC):
                S.step("t", lambda t, dq=dq, pb=pb:
                       t.transpose(psT[:, dq * 128:(dq + 1) * 128],
                                   concat_tl[pb][:, dq * 128:(dq + 1) * 128], sident),
                       *(("ts", 1) if dq == NDC - 1 else (None, 0)),
                       waits=(w if dq == 0 else ()),
                       tag=(f"tc{tt}" if dq == NDC - 1 else None))
            S.step("v", lambda v, pb=pb:
                   v.tensor_copy(attnT_tl[pb], psT[:, 0:1024]),
                   "vs", 1, waits=[f"tc{tt}"], tag=f"tc{tt}_e")
            bank_tag[7] = f"tc{tt}_e"

        def wo_resid_ln1(tt):
            pb = tt % 2
            for dq in range(NDC):
                for b in range(2):
                    w = []
                    if dq == 0:
                        w = [f"tc{tt}_e"] + [f"wo{d2}" for d2 in range(NDC) if tt == 0]
                        if tt == 0:
                            w += [f"nd{HP-1}_done"]
                        w += [bank_tag.get(f"wo{b}_bank")] if f"wo{b}_bank" in bank_tag else []
                    mm(None, psB[b][:],
                       attnT_tl[pb][:, dq * 128:(dq + 1) * 128],
                       wo_sb[:, dq * D + b * 512: dq * D + b * 512 + 512],
                       start=(dq == 0), stop=(dq == NDC - 1),
                       waits=[x for x in w if x],
                       tag=(f"wo{tt}b{b}" if dq == NDC - 1 else None))
            for b in range(2):
                S.step("v", lambda v, b=b, tt=tt:
                       v.scalar_tensor_tensor(xtok_tl[tt % 2][:, b * 512:(b + 1) * 512],
                                              psB[b][:], 1.0,
                                              xtok_tl[tt % 2][:, b * 512:(b + 1) * 512],
                                              ALU.mult, ALU.add,
                                              accum_out=stats[:, 2 * tt + b:2 * tt + b + 1]),
                       "vs", 1, waits=[f"wo{tt}b{b}", f"xtok{tt}"], tag=f"r1{tt}b{b}")
                bank_tag[f"wo{b}_bank"] = f"r1{tt}b{b}"
                S.step("s", lambda s, b=b, tt=tt:
                       s.activation(tmp[2], xtok_tl[tt % 2][:, b * 512:(b + 1) * 512],
                                    AF.Square,
                                    accum_out=stats[:, 32 + 2 * tt + b:32 + 2 * tt + b + 1]),
                       "ss", 1, waits=[f"r1{tt}b{b}"], tag=f"sq{tt}b{b}")
            c_sum, c_sq = 2 * tt, 32 + 2 * tt
            c_m, c_v, c_nm = 64 + tt, 80 + tt, 96 + tt
            S.step("v", lambda v: v.tensor_tensor(stats[:, c_m:c_m + 1],
                                                  stats[:, c_sum:c_sum + 1],
                                                  stats[:, c_sum + 1:c_sum + 2], ALU.add),
                   "vs", 1, waits=[f"r1{tt}b0", f"r1{tt}b1"], tag=f"ln1a{tt}", fence=True)
            S.step("v", lambda v: v.tensor_scalar(stats[:, c_m:c_m + 1],
                                                  stats[:, c_m:c_m + 1], 1.0 / D, None,
                                                  ALU.mult),
                   "vs", 1, tag=f"ln1b{tt}", fence=True)
            S.step("v", lambda v: v.tensor_tensor(stats[:, c_v:c_v + 1],
                                                  stats[:, c_sq:c_sq + 1],
                                                  stats[:, c_sq + 1:c_sq + 2], ALU.add),
                   "vs", 1, waits=[f"sq{tt}b0", f"sq{tt}b1"], tag=f"ln1c{tt}", fence=True)
            S.step("v", lambda v: v.tensor_scalar(stats[:, c_v:c_v + 1],
                                                  stats[:, c_v:c_v + 1], 1.0 / D, None,
                                                  ALU.mult),
                   "vs", 1, tag=f"ln1d{tt}", fence=True)
            S.step("v", lambda v: v.tensor_tensor(stats[:, 112:113], stats[:, c_m:c_m + 1],
                                                  stats[:, c_m:c_m + 1], ALU.mult),
                   "vs", 1, tag=f"ln1e{tt}", fence=True)
            S.step("v", lambda v: v.tensor_tensor(stats[:, c_v:c_v + 1],
                                                  stats[:, c_v:c_v + 1],
                                                  stats[:, 112:113], ALU.subtract),
                   "vs", 1, tag=f"ln1f{tt}", fence=True)
            S.step("s", lambda s: s.activation(stats[:, c_v:c_v + 1], stats[:, c_v:c_v + 1],
                                               AF.Ln, bias=EPS_LN),
                   "ss", 1, waits=[f"ln1f{tt}"], tag=f"ln1g{tt}", fence=True)
            S.step("s", lambda s: s.activation(stats[:, c_v:c_v + 1], stats[:, c_v:c_v + 1],
                                               AF.Exp, scale=-0.5),
                   "ss", 1, tag=f"ln1h{tt}", fence=True)
            S.step("v", lambda v: v.scalar_tensor_tensor(stats[:, c_nm:c_nm + 1],
                                                         stats[:, c_m:c_m + 1], -1.0,
                                                         stats[:, c_v:c_v + 1],
                                                         ALU.mult, ALU.mult),
                   "vs", 1, waits=[f"ln1h{tt}"], tag=f"ln1i{tt}", fence=True)
            S.step("v", lambda v, tt=tt:
                   v.tensor_scalar(xtok_tl[tt % 2], xtok_tl[tt % 2],
                                   stats[:, c_v:c_v + 1], stats[:, c_nm:c_nm + 1],
                                   ALU.mult, ALU.add),
                   "vs", 1, tag=f"y1{tt}")
            S.step("g", lambda g, tt=tt: g.dma_start(y1_d[tt * 128:(tt + 1) * 128, :],
                                                     xtok_tl[tt % 2]),
                   "ld", 16, waits=[f"y1{tt}"], tag=f"y1sp{tt}")
            c_mx, c_sx, c_rx = 112, 128 + tt, 144 + tt
            S.step("v", lambda v: v.tensor_reduce(stats[:, c_mx:c_mx + 1], xtok_tl[tt % 2],
                                                  AX.X, ALU.max, apply_absolute_value=True),
                   "vs", 1, waits=[f"y1{tt}"], tag=f"xqa{tt}", fence=True)
            S.step("v", lambda v: v.tensor_scalar(stats[:, c_mx:c_mx + 1],
                                                  stats[:, c_mx:c_mx + 1], EPS_LN, None,
                                                  ALU.max),
                   "vs", 1, tag=f"xqb{tt}", fence=True)
            S.step("v", lambda v: v.tensor_copy(stats[:, c_mx + 1:c_mx + 2],
                                                stats[:, c_mx:c_mx + 1]),
                   "vs", 1, tag=f"xqc{tt}", fence=True)
            S.step("v", lambda v: v.reciprocal(stats[:, 114:116], stats[:, c_mx:c_mx + 2]),
                   "vs", 1, tag=f"xqd{tt}", fence=True)
            S.step("v", lambda v: v.tensor_scalar(stats[:, c_sx:c_sx + 1], stats[:, 114:115],
                                                  127.0, None, ALU.mult),
                   "vs", 1, tag=f"xqe{tt}", fence=True)
            S.step("v", lambda v: v.tensor_copy(stats[:, 114:115], stats[:, c_sx:c_sx + 1]),
                   "vs", 1, tag=f"xqf{tt}", fence=True)
            S.step("v", lambda v: v.reciprocal(stats[:, 116:118], stats[:, 114:116]),
                   "vs", 1, tag=f"xqg{tt}", fence=True)
            S.step("v", lambda v: v.tensor_copy(stats[:, c_rx:c_rx + 1], stats[:, 116:117]),
                   "vs", 1, tag=f"xqh{tt}", fence=True)
            S.step("v", lambda v, tt=tt: v.tensor_scalar(i32st[:, 0:1024], xtok_tl[tt % 2],
                                                         stats[:, c_sx:c_sx + 1], None,
                                                         ALU.mult),
                   "vs", 1, tag=f"xqi{tt}")
            S.step("v", lambda v: v.tensor_scalar(xq_stage, i32st[:, 0:1024],
                                                  127.0, -128.0, ALU.min, ALU.max),
                   "vs", 1, waits=([f"tx{tt - 1}"] if tt >= 1 else []), tag=f"xqj{tt}")

        def transpose_xq(tt):
            w = [f"xqj{tt}", bank_tag[7]]
            for dc in range(NDC):
                S.step("t", lambda t, dc=dc:
                       t.transpose(psT[:, dc * 128:(dc + 1) * 128],
                                   xq_stage[:, dc * 128:(dc + 1) * 128], sident),
                       *(("ts", 1) if dc == NDC - 1 else (None, 0)),
                       waits=(w if dc == 0 else ()),
                       tag=(f"tx{tt}" if dc == NDC - 1 else None))
            S.step("v", lambda v, tt=tt:
                   v.tensor_copy(xqT[:].rearrange("p (dc t) -> p dc t", dc=NDC)
                                 [:, :, tt * 128:(tt + 1) * 128],
                                 psT[:, 0:1024].rearrange("p (dc c) -> p dc c", dc=NDC)),
                   "vs", 1, waits=[f"tx{tt}"], tag=f"tx{tt}_e")
            bank_tag[7] = f"tx{tt}_e"

        def xtok_load(tt):
            w = [f"y1sp{tt - 2}"] if tt >= 2 else []
            S.step("y", lambda y, tt=tt:
                   y.dma_start(xtok_tl[tt % 2], x_tok[tt * 128:(tt + 1) * 128, :]),
                   "ls", 16, waits=w, tag=f"xtok{tt}")

        xtok_load(0)
        xtok_load(1)
        for tt in range(NTT):
            num_den(tt)
            transpose_concat(tt)
            if tt >= 1:
                transpose_xq(tt - 1)
            wo_resid_ln1(tt)
            if tt + 2 < NTT:
                xtok_load(tt + 2)
        transpose_xq(NTT - 1)

        wsum_reduce(1, "u")
        wsum_reduce(2, "d")
        AR.release("qf", "nd", "wo_sb", "dump", "cc0", "cc1", "at0", "at1",
                   "xq_stage", "kf0", "kf1", "v0", "v1", "xtok0", "xtok1",
                   "tmp3")

        tern_g = [AR.alloc(f"tern_g{i}", 4 * INNER, FP8) for i in range(2)]
        tern_u = [AR.alloc(f"tern_u{i}", 4 * INNER, FP8) for i in range(2)]
        tern_d = [AR.alloc(f"tern_d{i}", 16 * D, FP8) for i in range(2)]
        TERN = {"g": tern_g, "u": tern_u}

        def tern_gu_sl(mat, dc, off, n):
            seg = TERN[mat][dc // 4]
            base = (dc % 4) * INNER + off
            return seg[:, base:base + n]

        def tern_d_sl(it, off, n):
            seg = tern_d[it // 16]
            base = (it % 16) * D + off
            return seg[:, base:base + n]

        def p2_consume(si, ent):
            mat = ent[0]
            buf = si % 2
            S.step("v", lambda v, buf=buf, mat=mat:
                   v.tensor_scalar(i32st, wstage[buf], stats[:, SW[mat]:SW[mat] + 1],
                                   None, ALU.mult),
                   "vs", 1, waits=[f"p2L{si}", f"wr{mat}_sw"], tag=f"p2q{si}a")
            if mat in ("g", "u"):
                _, ti, h = ent
                dst = tern_gu_sl(mat, ti, h * 2048, 2048)
            else:
                _, sd, _ = ent
                dst = tern_d_sl(2 * sd, 0, 2048)
            S.step("v", lambda v, dst=dst:
                   v.tensor_scalar(dst, i32st, 1.0, -1.0, ALU.min, ALU.max),
                   "vs", 1, tag=f"p2q{si}b")

        for si, ent in enumerate(stage_list):
            if si >= 2:
                p2_consume(si - 2, stage_list[si - 2])
            S.step("y", stage_load_fn(si % 2, ent), "ls", 16,
                   waits=(["p1c47"] if si < 2 else [f"p2q{si - 2}b"]), tag=f"p2L{si}")
        p2_consume(46, stage_list[46])
        p2_consume(47, stage_list[47])

        y1c_tl = [AR.alloc("y1c0", D, F32), AR.alloc("y1c1", D, F32)]
        gateh_seg = [AR.alloc(f"gateh{i}", 2048, F32) for i in range(2)]
        hq = AR.alloc("hq", INNER, BF16)
        hqT_seg = [AR.alloc("hqT0", INNER // 2, BF16), AR.alloc("hqT1", INNER // 2, BF16)]

        def hqT_sl(col, n):
            seg = col // 2048
            return hqT_seg[seg][:, col - seg * 2048: col - seg * 2048 + n]
        y2_seg = [AR.alloc("y2a", D // 2, F32), AR.alloc("y2b", D // 2, F32)]

        def gateh_sl(oc, n=512):
            return gateh_seg[oc // 4][:, (oc % 4) * 512:(oc % 4) * 512 + n]

        def ffn_front(c):
            cb = 0
            ycb = c % 2
            S.step("y", lambda y, c=c, ycb=ycb: y.dma_start(y1c_tl[ycb],
                                                            y1_d[c * 128:(c + 1) * 128, :]),
                   "ls", 16,
                   waits=[f"y1sp{c}"] + ([f"r2{c - 2}b1"] if c >= 2 else []),
                   tag=f"y1r{c}")
            cgc = 204 + 2 * (c % 2)
            S.step("v", lambda v, c=c, cgc=cgc: v.tensor_tensor(stats[:, cgc:cgc + 1],
                                                       stats[:, 144 + c:145 + c],
                                                       stats[:, MCLIP['g']:MCLIP['g'] + 1],
                                                       ALU.mult),
                   "vs", 1,
                   waits=[f"wrg_sw", f"xqh{c}"] +
                         ([f"ge{c - 2}w1i3"] if c >= 2 else []),
                   tag=f"cg{c}", fence=True)
            S.step("v", lambda v, c=c: v.tensor_tensor(stats[:, 185:186],
                                                       stats[:, 144 + c:145 + c],
                                                       stats[:, MCLIP['u']:MCLIP['u'] + 1],
                                                       ALU.mult),
                   "vs", 1, waits=[f"wru_sw"], tag=f"cu{c}", fence=True)

            for wave in range(2):
                for dc in range(NDC):
                    for i in range(4):
                        oc = wave * 4 + i
                        w = []
                        if dc == 0:
                            w = [bank_tag[i]] if i in bank_tag else []
                            if c == 0 and wave == 0:
                                w += [f"tx{NTT-1}_e"]
                            if i == 0:
                                w += [f"p2q{2 * t2 + (1 if oc >= 4 else 0)}b"
                                      for t2 in range(NDC)] if c == 0 else []
                        mm(i, psA[i][:],
                           xqT[:, dc * T + c * 128: dc * T + c * 128 + 128],
                           tern_gu_sl("g", dc, oc * 512, 512),
                           start=(dc == 0), stop=(dc == NDC - 1),
                           waits=w, tag=(f"g{c}w{wave}i{i}" if dc == NDC - 1 else None))
                for i in range(4):
                    oc = wave * 4 + i
                    S.step("s", lambda s, i=i, oc=oc, c=c, cgc=cgc:
                           s.activation(gateh_sl(oc), psA[i][:],
                                        AF.Silu, scale=stats[:, cgc:cgc + 1]),
                           "ss", 1,
                           waits=[f"g{c}w{wave}i{i}", f"cg{c}"] +
                                 ([f"hqa{c - 1}_1"] if (c >= 1 and wave == 0 and i == 0)
                                  else []),
                           tag=f"ge{c}w{wave}i{i}")
                    bank_tag[i] = f"ge{c}w{wave}i{i}"
            for wave in range(2):
                for dc in range(NDC):
                    for i in range(4):
                        oc = wave * 4 + i
                        w = []
                        if dc == 0:
                            w = [bank_tag[i]] if i in bank_tag else []
                            if i == 0 and c == 0:
                                w += [f"p2q{16 + 2 * t2 + (1 if oc >= 4 else 0)}b"
                                      for t2 in range(NDC)]
                        mm(i, psA[i][:],
                           xqT[:, dc * T + c * 128: dc * T + c * 128 + 128],
                           tern_gu_sl("u", dc, oc * 512, 512),
                           start=(dc == 0), stop=(dc == NDC - 1),
                           waits=w, tag=(f"u{c}w{wave}i{i}" if dc == NDC - 1 else None))
                for i in range(4):
                    oc = wave * 4 + i
                    S.step("v", lambda v, i=i, oc=oc, c=c:
                           v.scalar_tensor_tensor(gateh_sl(oc),
                                                  psA[i][:], stats[:, 185:186],
                                                  gateh_sl(oc),
                                                  ALU.mult, ALU.mult),
                           "vs", 1,
                           waits=[f"u{c}w{wave}i{i}", f"ge{c}w{wave}i{i}", f"cu{c}"],
                           tag=f"h{c}w{wave}i{i}")
                    bank_tag[i] = f"h{c}w{wave}i{i}"

            hw = [f"h{c}w{wv}i{i}" for wv in range(2) for i in range(4)]
            S.step("v", lambda v: v.tensor_reduce(stats[:, 186:187], gateh_seg[0],
                                                  AX.X, ALU.max, apply_absolute_value=True),
                   "vs", 1, waits=hw, tag=f"hm0{c}", fence=True)
            S.step("v", lambda v: v.tensor_reduce(stats[:, 187:188], gateh_seg[1],
                                                  AX.X, ALU.max, apply_absolute_value=True),
                   "vs", 1, tag=f"hm1{c}", fence=True)
            S.step("v", lambda v: v.tensor_tensor(stats[:, 188:189], stats[:, 186:187],
                                                  stats[:, 187:188], ALU.max),
                   "vs", 1, tag=f"hm2{c}", fence=True)
            S.step("v", lambda v: v.tensor_scalar(stats[:, 188:189], stats[:, 188:189],
                                                  EPS_LN, None, ALU.max),
                   "vs", 1, tag=f"hm3{c}", fence=True)
            S.step("v", lambda v: v.tensor_copy(stats[:, 189:190], stats[:, 188:189]),
                   "vs", 1, tag=f"hm4{c}", fence=True)
            S.step("v", lambda v: v.reciprocal(stats[:, 190:192], stats[:, 188:190]),
                   "vs", 1, tag=f"hm5{c}", fence=True)
            S.step("v", lambda v: v.tensor_scalar(stats[:, 192:193], stats[:, 190:191],
                                                  127.0, None, ALU.mult),
                   "vs", 1, tag=f"hm6{c}", fence=True)
            S.step("v", lambda v: v.tensor_copy(stats[:, 190:191], stats[:, 192:193]),
                   "vs", 1, tag=f"hm7{c}", fence=True)
            S.step("v", lambda v: v.reciprocal(stats[:, 193:195], stats[:, 190:192]),
                   "vs", 1, tag=f"hm8{c}", fence=True)
            cdc = 208 + (c % 2)
            S.step("v", lambda v, cdc=cdc: v.tensor_tensor(stats[:, cdc:cdc + 1],
                                                           stats[:, 193:194],
                                                           stats[:, MCLIP['d']:MCLIP['d'] + 1],
                                                           ALU.mult),
                   "vs", 1, waits=[f"wrd_sw"], tag=f"cd{c}", fence=True)
            for hh in range(2):
                S.step("v", lambda v, hh=hh:
                       v.tensor_scalar(i32st, gateh_seg[hh],
                                       stats[:, 192:193], None, ALU.mult),
                       "vs", 1, tag=f"hqa{c}_{hh}")
                S.step("v", lambda v, hh=hh:
                       v.tensor_scalar(hq[:, hh * 2048:(hh + 1) * 2048], i32st,
                                       127.0, -128.0, ALU.min, ALU.max),
                       "vs", 1, tag=f"hqb{c}_{hh}")

        def ffn_back_T(c):
            cb = 0
            for bt in range(4):
                w = [f"hqb{c}_{bt // 2}", bank_tag[7]]
                for j in range(8):
                    it = bt * 8 + j
                    S.step("t", lambda t, it=it, j=j:
                           t.transpose(psT[:, j * 128:(j + 1) * 128],
                                       hq[:, it * 128:(it + 1) * 128], sident),
                           *(("ts", 1) if j == 7 else (None, 0)),
                           waits=(w if j == 0 else ()),
                           tag=(f"th{c}_{bt}" if j == 7 else None))
                S.step("v", lambda v, bt=bt:
                       v.tensor_copy(hqT_sl(bt * 1024, 1024), psT[:, 0:1024]),
                       "vs", 1, waits=[f"th{c}_{bt}"], tag=f"th{c}_{bt}_e")
                bank_tag[7] = f"th{c}_{bt}_e"

        def ffn_back_D(c):
            cb = 0
            ycb = c % 2
            for it in range(32):
                for b in range(2):
                    w = []
                    if it == 0:
                        w = [f"th{c}_0_e"]
                        w += [f"r2{c - 1}b{b}"] if c >= 1 else []
                        if c == 0:
                            w += [f"p2q{32 + sd}b" for sd in range(16)]
                            w += [f"r1{NTT-1}b{b}"]
                    elif it % 8 == 0:
                        w = [f"th{c}_{it // 8}_e"]
                    mm(None, psB[b][:],
                       hqT_sl(it * 128, 128),
                       tern_d_sl(it, b * 512, 512),
                       start=(it == 0), stop=(it == 31),
                       waits=w, tag=(f"d{c}b{b}" if it == 31 else None))
            for b in range(2):
                cdc = 208 + (c % 2)
                S.step("v", lambda v, b=b, c=c, ycb=ycb, cdc=cdc:
                       v.scalar_tensor_tensor(y2_seg[b][:],
                                              psB[b][:], stats[:, cdc:cdc + 1],
                                              y1c_tl[ycb][:, b * 512:(b + 1) * 512],
                                              ALU.mult, ALU.add,
                                              accum_out=stats[:, 196 + b:197 + b]),
                       "vs", 1,
                       waits=[f"d{c}b{b}", f"y1r{c}", f"cd{c}"] +
                             ([f"yst{c - 1}s{b}"] if c >= 1 else []),
                       tag=f"r2{c}b{b}")
                S.step("s", lambda s, b=b, c=c:
                       s.activation(tmp[2], y2_seg[b][:],
                                    AF.Square, accum_out=stats[:, 198 + b:199 + b]),
                       "ss", 1, waits=[f"r2{c}b{b}"], tag=f"sq2{c}b{b}")
            S.step("v", lambda v: v.tensor_tensor(stats[:, 200:201], stats[:, 196:197],
                                                  stats[:, 197:198], ALU.add),
                   "vs", 1, waits=[f"r2{c}b0", f"r2{c}b1"], tag=f"l2a{c}", fence=True)
            S.step("v", lambda v: v.tensor_scalar(stats[:, 200:201], stats[:, 200:201],
                                                  1.0 / D, None, ALU.mult),
                   "vs", 1, tag=f"l2b{c}", fence=True)
            S.step("v", lambda v: v.tensor_tensor(stats[:, 201:202], stats[:, 198:199],
                                                  stats[:, 199:200], ALU.add),
                   "vs", 1, waits=[f"sq2{c}b0", f"sq2{c}b1"], tag=f"l2c{c}", fence=True)
            S.step("v", lambda v: v.tensor_scalar(stats[:, 201:202], stats[:, 201:202],
                                                  1.0 / D, None, ALU.mult),
                   "vs", 1, tag=f"l2d{c}", fence=True)
            S.step("v", lambda v: v.tensor_tensor(stats[:, 202:203], stats[:, 200:201],
                                                  stats[:, 200:201], ALU.mult),
                   "vs", 1, tag=f"l2e{c}", fence=True)
            S.step("v", lambda v: v.tensor_tensor(stats[:, 201:202], stats[:, 201:202],
                                                  stats[:, 202:203], ALU.subtract),
                   "vs", 1, tag=f"l2f{c}", fence=True)
            S.step("s", lambda s: s.activation(stats[:, 201:202], stats[:, 201:202],
                                               AF.Ln, bias=EPS_LN),
                   "ss", 1, waits=[f"l2f{c}"], tag=f"l2g{c}", fence=True)
            S.step("s", lambda s: s.activation(stats[:, 201:202], stats[:, 201:202],
                                               AF.Exp, scale=-0.5),
                   "ss", 1, tag=f"l2h{c}", fence=True)
            S.step("v", lambda v: v.scalar_tensor_tensor(stats[:, 203:204],
                                                         stats[:, 200:201], -1.0,
                                                         stats[:, 201:202],
                                                         ALU.mult, ALU.mult),
                   "vs", 1, waits=[f"l2h{c}"], tag=f"l2i{c}", fence=True)
            for b in range(2):
                S.step("v", lambda v, b=b: v.tensor_scalar(y2_seg[b][:], y2_seg[b][:],
                                                           stats[:, 201:202],
                                                           stats[:, 203:204],
                                                           ALU.mult, ALU.add),
                       "vs", 1, tag=f"y2{c}s{b}")
                S.step("g", lambda g, c=c, b=b:
                       g.dma_start(y_d[c * 128:(c + 1) * 128, b * 512:(b + 1) * 512],
                                   y2_seg[b][:]),
                       "ld", 16, waits=[f"y2{c}s{b}"], tag=f"yst{c}s{b}")

        ffn_front(0)
        for c in range(1, NTT):
            ffn_back_T(c - 1)
            ffn_front(c)
            ffn_back_D(c - 1)
        ffn_back_T(NTT - 1)
        ffn_back_D(NTT - 1)

        tot_ld = S.totals.get("ld", 0)
        tot_ls = S.totals.get("ls", 0)

        S.check_deadlock()
        block = ctx.enter_context(nc.Block())

        @block.tensor
        def _(t):
            S.emit("t", t, sems)

        @block.vector
        def _(v):
            S.emit("v", v, sems)

        @block.scalar
        def _(s):
            S.emit("s", s, sems)

        @block.gpsimd
        def _(g):
            S.emit("g", g, sems)

        @block.sync
        def _(sy):
            S.emit("y", sy, sems)
            sy.wait_ge(sems["ld"], tot_ld)
            sy.wait_ge(sems["ls"], tot_ls)

    return nc


_NC_CACHE = None


def kernel(L0_attn_x, L0_attn_Wq, L0_attn_Wk, L0_attn_Wv, L0_attn_Wo,
           L0_ffn_w_gate, L0_ffn_w_up, L0_ffn_w_down):
    global _NC_CACHE
    x = np.ascontiguousarray(np.asarray(L0_attn_x, dtype=np.float32))
    wq = np.ascontiguousarray(np.asarray(L0_attn_Wq, dtype=np.float32))
    wk = np.ascontiguousarray(np.asarray(L0_attn_Wk, dtype=np.float32))
    wv = np.ascontiguousarray(np.asarray(L0_attn_Wv, dtype=np.float32))
    wo = np.ascontiguousarray(np.asarray(L0_attn_Wo, dtype=np.float32))
    wtg = np.ascontiguousarray(np.asarray(L0_ffn_w_gate, dtype=np.float32).T)
    wtu = np.ascontiguousarray(np.asarray(L0_ffn_w_up, dtype=np.float32).T)
    wtd = np.ascontiguousarray(np.asarray(L0_ffn_w_down, dtype=np.float32).T)
    ident = np.eye(128, dtype=np.float32)

    if _NC_CACHE is None:
        _NC_CACHE = build_nc()
    nc = _NC_CACHE

    in_maps = []
    for b in range(8):
        xb = np.ascontiguousarray(x[b])
        in_maps.append(dict(x_tok=xb, x_feat=np.ascontiguousarray(xb.T),
                            wq=wq, wk=wk, wv=wv, wo=wo,
                            wtg=wtg, wtu=wtu, wtd=wtd, ident=ident))
    res = run_bass_kernel_spmd(nc, in_maps, core_ids=list(range(8)))
    out = np.stack([np.asarray(res.results[b]["y"], dtype=np.float32) for b in range(8)])
    return out


# version 06 (final)
# speedup vs baseline: 1.1011x; 1.0073x over previous
import numpy as np
import ml_dtypes
from contextlib import ExitStack

import concourse.bass as bass
import concourse.mybir as mybir
from concourse.bass_utils import run_bass_kernel_spmd

F32 = mybir.dt.float32
BF16 = mybir.dt.bfloat16
I32 = mybir.dt.int32
FP8 = mybir.dt.float8e4
I8 = mybir.dt.int8
AF = mybir.ActivationFunctionType
ALU = mybir.AluOpType
AX = mybir.AxisListType

T = 2048
D = 1024
HP = 8
INNER = 4096
NTT = 16
NDC = 8
EPS_ATTN = 1e-6
EPS_LN = 1e-5

ARENA_BYTES = 207 * 1024


class Sched:

    ENGS = ("t", "v", "s", "g", "y")

    def __init__(self):
        self.steps = []
        self.tags = {}
        self.totals = {}

    def step(self, eng, fn, sem=None, inc=0, waits=(), tag=None, fence=False):
        assert eng in self.ENGS, eng
        for w in waits:
            assert w in self.tags, f"wait on undefined tag {w!r}"
        val = None
        if sem is not None and inc:
            self.totals[sem] = self.totals.get(sem, 0) + inc
            val = self.totals[sem]
            if tag is not None:
                assert tag not in self.tags, f"duplicate tag {tag!r}"
                self.tags[tag] = (sem, val)
        else:
            assert tag is None and not fence
        self.steps.append((eng, fn, sem, inc, val, tuple(waits), fence))

    def check_deadlock(self):
        import bisect
        from collections import defaultdict, deque
        achieve = {}
        for idx, (eng, fn, sem, inc, val, waits, fence) in enumerate(self.steps):
            if sem is not None and inc:
                achieve.setdefault(sem, []).append((val, idx))
        def achiever(sem, v):
            lst = achieve[sem]
            i = bisect.bisect_left([x[0] for x in lst], v)
            return lst[i][1] if i < len(lst) else None
        n = len(self.steps)
        edges = defaultdict(list)
        last = {}
        for idx, (eng, fn, sem, inc, val, waits, fence) in enumerate(self.steps):
            if eng in last:
                edges[last[eng]].append(idx)
            last[eng] = idx
            for w in waits:
                ws, wv = self.tags[w]
                a = achiever(ws, wv)
                if a is not None:
                    edges[a].append(idx)
        indeg = [0] * n
        for u, vs in edges.items():
            for v in vs:
                indeg[v] += 1
        q = deque(i for i in range(n) if indeg[i] == 0)
        seen = 0
        while q:
            u = q.popleft()
            seen += 1
            for v in edges[u]:
                indeg[v] -= 1
                if indeg[v] == 0:
                    q.append(v)
        assert seen == n, f"schedule deadlock: {n - seen} steps in a cycle"

    def emit(self, eng_name, eng_obj, sems):
        wm = {}
        for (eng, fn, sem, inc, val, waits, fence) in self.steps:
            if eng != eng_name:
                continue
            for w in waits:
                wsem, wval = self.tags[w]
                if wm.get(wsem, 0) < wval:
                    eng_obj.wait_ge(sems[wsem], wval)
                    wm[wsem] = wval
            inst = fn(eng_obj)
            if sem is not None and inc:
                inst.then_inc(sems[sem], inc)
                if fence:
                    eng_obj.wait_ge(sems[sem], val)
                    wm[sem] = max(wm.get(sem, 0), val)


class Arena:

    def __init__(self, tensor):
        self.t = tensor
        self.free = [(0, ARENA_BYTES)]
        self.live = {}

    def alloc(self, name, cols, dt):
        dsz = {F32: 4, BF16: 2, I32: 4, FP8: 1}[dt]
        n = (cols * dsz + 63) // 64 * 64
        for i, (off, sz) in enumerate(self.free):
            if sz >= n:
                self.free[i] = (off + n, sz - n)
                if self.free[i][1] == 0:
                    self.free.pop(i)
                self.live[name] = (off, n)
                return self.t[:, off:off + cols * dsz].bitcast(dt)
        raise MemoryError(f"SBUF arena OOM for {name} ({n}B); free={self.free}")

    def release(self, *names):
        for name in names:
            off, n = self.live.pop(name)
            self.free.append((off, n))
        self.free.sort()
        merged = []
        for o, s in self.free:
            if merged and merged[-1][0] + merged[-1][1] == o:
                merged[-1] = [merged[-1][0], merged[-1][1] + s]
            else:
                merged.append([o, s])
        self.free = [tuple(x) for x in merged]


def build_nc():
    nc = bass.Bass("TRN2", target_bir_lowering=False, debug=False)

    x_tok = nc.dram_tensor("x_tok", [T, D], F32, kind="ExternalInput").ap()
    x_feat = nc.dram_tensor("x_feat", [D, T], F32, kind="ExternalInput").ap()
    wq_d = nc.dram_tensor("wq", [D, D], F32, kind="ExternalInput").ap()
    wk_d = nc.dram_tensor("wk", [D, D], F32, kind="ExternalInput").ap()
    wv_d = nc.dram_tensor("wv", [D, D], F32, kind="ExternalInput").ap()
    wo_d = nc.dram_tensor("wo", [D, D], F32, kind="ExternalInput").ap()
    wtg_d = nc.dram_tensor("wtg", [D, INNER], F32, kind="ExternalInput").ap()
    wtu_d = nc.dram_tensor("wtu", [D, INNER], F32, kind="ExternalInput").ap()
    wtd_d = nc.dram_tensor("wtd", [INNER, D], F32, kind="ExternalInput").ap()
    ident_d = nc.dram_tensor("ident", [128, 128], F32, kind="ExternalInput").ap()
    y_d = nc.dram_tensor("y", [T, D], F32, kind="ExternalOutput").ap()
    y1_d = nc.dram_tensor("y1scratch", [T, D], F32, kind="Internal").ap()

    S = Sched()

    with ExitStack() as ctx:
        arena_t = ctx.enter_context(nc.sbuf_tensor("arena", [128, ARENA_BYTES], I8))
        AR = Arena(arena_t)
        onesrow = ctx.enter_context(nc.sbuf_tensor("onesrow", [1, 128], F32))

        psA = [ctx.enter_context(nc.psum_tensor(f"psA{i}", [128, 512], F32)) for i in range(4)]
        psB = [ctx.enter_context(nc.psum_tensor(f"psB{i}", [128, 512], F32)) for i in range(3)]
        psT = ctx.enter_context(nc.psum_tensor("psT", [128, 1024], BF16))

        sems = {n: ctx.enter_context(nc.semaphore("sem_" + n))
                for n in ("ld", "ls", "ts", "vs", "ss", "gs")}

        sident = AR.alloc("ident", 128, BF16)
        ones_bf = AR.alloc("ones_bf", 2, BF16)
        ones_f = AR.alloc("ones_f", 2, F32)
        consts = AR.alloc("consts", 16, F32)
        nc.const_aps.register(F32, EPS_LN, consts[:, 0:1])
        stats = AR.alloc("stats", 256, F32)
        wsums = AR.alloc("wsums", 96, F32)
        rc = AR.alloc("rc", 16, F32)
        tmp = [AR.alloc(f"tmp{i}", 512, F32) for i in range(4)]

        MCLIP = {"g": 168, "u": 170, "d": 172}
        SW = {"g": 176, "u": 178, "d": 180}

        S.step("g", lambda g: g.dma_start(sident, ident_d), "ld", 16, tag="ident")
        S.step("g", lambda g: g.memset(ones_bf[:, 0:1], 1.0), "gs", 1, tag="ones_bf")
        S.step("g", lambda g: g.memset(ones_f[:, 0:1], 1.0), "gs", 1, tag="ones_f")
        S.step("g", lambda g: g.memset(onesrow[:], 1.0), "gs", 1, tag="ones_row")
        S.step("g", lambda g: g.memset(consts[:, 0:1], EPS_LN), "gs", 1, tag="c_eps")

        xT = AR.alloc("xT", NDC * T, BF16)
        wk_sb = AR.alloc("wk_sb", NDC * D, BF16)
        wv_sb = AR.alloc("wv_sb", NDC * D, BF16)
        wq_sb = AR.alloc("wq_sb", NDC * D, BF16)
        for dc in range(NDC):
            S.step("g", lambda g, dc=dc: g.dma_start(xT[:, dc * T:(dc + 1) * T],
                                                     x_feat[dc * 128:(dc + 1) * 128, :]),
                   "ld", 16, tag=f"xT{dc}")
        for nm, dst, src in (("wk", wk_sb, wk_d), ("wv", wv_sb, wv_d), ("wq", wq_sb, wq_d)):
            for dc in range(NDC):
                S.step("g", lambda g, dst=dst, src=src, dc=dc:
                       g.dma_start(dst[:, dc * D:(dc + 1) * D], src[dc * 128:(dc + 1) * 128, :]),
                       "ld", 16, tag=f"{nm}{dc}")

        wstage = [AR.alloc("wstage0", 2048, F32), AR.alloc("wstage1", 2048, F32)]
        stage_list = ([("g", ti, h) for ti in range(NDC) for h in range(2)] +
                      [("u", ti, h) for ti in range(NDC) for h in range(2)] +
                      [("d", sd, 0) for sd in range(16)])

        def stage_load_fn(buf, ent):
            mat = ent[0]
            if mat in ("g", "u"):
                _, ti, h = ent
                src = wtg_d if mat == "g" else wtu_d
                return lambda y: y.dma_start(wstage[buf],
                                             src[ti * 128:(ti + 1) * 128,
                                                 h * 2048:(h + 1) * 2048])
            _, sd, _ = ent
            return lambda y: y.dma_start(
                wstage[buf][:].rearrange("p (a b) -> p a b", a=2),
                wtd_d[:].rearrange("(a p) b -> p a b", p=128)[:, 2 * sd:2 * sd + 2, :])

        xtok_tl = [AR.alloc("xtok0", D, F32), AR.alloc("xtok1", D, F32)]
        dump = AR.alloc("dump", 1024, F32)

        def p1_consume(si):
            for q in range(2):
                S.step("s", lambda s, buf=si % 2, q=q, si=si:
                       s.activation(dump, wstage[buf][:, q * 1024:(q + 1) * 1024],
                                    AF.Abs, accum_out=wsums[:, 2 * si + q:2 * si + q + 1]),
                       "ss", 1, waits=[f"p1L{si}"], tag=(f"p1c{si}" if q == 1 else None))

        for si, ent in enumerate(stage_list):
            if si >= 2:
                p1_consume(si - 2)
            waits = [f"p1c{si - 2}"] if si >= 2 else []
            S.step("y", stage_load_fn(si % 2, ent), "ls", 16, waits=waits, tag=f"p1L{si}")
        p1_consume(46)
        p1_consume(47)

        wo_sb = AR.alloc("wo_sb", NDC * D, BF16)
        for dc in range(NDC):
            S.step("g", lambda g, dc=dc: g.dma_start(wo_sb[:, dc * D:(dc + 1) * D],
                                                     wo_d[dc * 128:(dc + 1) * 128, :]),
                   "ld", 16, tag=f"wo{dc}")

        bank_tag = {}

        def mm(bank_idx, psum_ap, lhsT, rhs, start, stop, waits=(), tag=None):
            S.step("t", lambda t, l=lhsT, r=rhs, o=psum_ap, a=start, z=stop:
                   t.matmul(o, l, r, start=a, stop=z, skip_group_check=True),
                   *(("ts", 1) if tag else (None, 0)), waits=waits, tag=tag)

        def phi_evac(key, psum_ap, dst_ap, pp, extra_waits=()):
            t0, t1 = tmp[2 * pp], tmp[2 * pp + 1]
            w = [key] + list(extra_waits)
            S.step("v", lambda v: v.tensor_scalar(t0, psum_ap, 0.0, None, ALU.min),
                   "vs", 1, waits=w, tag=key + "_t0")
            S.step("s", lambda s: s.activation(t1, t0, AF.Exp),
                   "ss", 1, waits=[key + "_t0"], tag=key + "_t1")
            S.step("v", lambda v: v.scalar_tensor_tensor(dst_ap, psum_ap, 0.0, t1,
                                                         ALU.max, ALU.add),
                   "vs", 1, waits=[key + "_t1"], tag=key + "_done")

        kf_tl = [AR.alloc("kf0", D, BF16), AR.alloc("kf1", D, BF16)]
        v_tl = [AR.alloc("v0", D, BF16), AR.alloc("v1", D, BF16)]
        qf = AR.alloc("qf", NDC * T, BF16)

        def proj_kv(tt):
            pb = tt % 2
            base_w = ([f"xT{dc}" for dc in range(NDC)] +
                      [f"wk{dc}" for dc in range(NDC)] + [f"wv{dc}" for dc in range(NDC)])
            for dc in range(NDC):
                for b, (wsb, oc) in enumerate(((wk_sb, 0), (wk_sb, 1), (wv_sb, 0), (wv_sb, 1))):
                    waits = []
                    if dc == 0:
                        waits = (base_w if tt == 0 else []) + \
                                ([bank_tag[b]] if b in bank_tag else []) + \
                                ([f"kvacc{tt - 2}"] if tt >= 2 and b == 0 else [])
                    mm(b, psA[b][:],
                       xT[:, dc * T + tt * 128: dc * T + tt * 128 + 128],
                       wsb[:, dc * D + oc * 512: dc * D + oc * 512 + 512],
                       start=(dc == 0), stop=(dc == NDC - 1), waits=waits,
                       tag=(f"p{tt}b{b}" if dc == NDC - 1 else None))
            for b in range(2):
                phi_evac(f"p{tt}b{b}", psA[b][:], kf_tl[pb][:, b * 512:(b + 1) * 512], b,
                         extra_waits=([f"kvacc{tt - 2}"] if tt >= 2 else []))
                bank_tag[b] = f"p{tt}b{b}_done"
            for b in (2, 3):
                S.step("s", lambda s, pb=pb, b=b:
                       s.copy(v_tl[pb][:, (b - 2) * 512:(b - 1) * 512], psA[b][:]),
                       "ss", 1,
                       waits=[f"p{tt}b{b}"] + ([f"kvacc{tt - 2}"] if tt >= 2 else []),
                       tag=f"p{tt}b{b}_done")
                bank_tag[b] = f"p{tt}b{b}_done"

        def kv_acc(tt):
            pb = tt % 2
            for hp in range(HP):
                pbank = psB[0] if hp < 4 else psB[1]
                off = (hp % 4) * 128
                w = [f"p{tt}b0_done", f"p{tt}b1_done", f"p{tt}b2_done", f"p{tt}b3_done"] \
                    if hp == 0 else []
                if tt == 0 and hp == 0:
                    w = w + ["ones_bf"]
                mm(None, pbank[:, off:off + 128],
                   kf_tl[pb][:, hp * 128:(hp + 1) * 128],
                   v_tl[pb][:, hp * 128:(hp + 1) * 128],
                   start=(tt == 0 and hp % 4 == 0), stop=False, waits=w)
                mm(None, psB[2][:, hp:hp + 1],
                   kf_tl[pb][:, hp * 128:(hp + 1) * 128],
                   ones_bf[:, 0:1],
                   start=(tt == 0 and hp == 0), stop=(tt == NTT - 1 and hp == HP - 1),
                   waits=(), tag=(f"kvacc{tt}" if hp == HP - 1 else None))

        for tt in range(NTT):
            proj_kv(tt)
            if tt >= 1:
                kv_acc(tt - 1)
        kv_acc(NTT - 1)
        AR.release("wk_sb", "wv_sb")

        for ot in range(NDC):
            for dc in range(NDC):
                for lc in range(4):
                    waits = []
                    if dc == 0:
                        waits = ([bank_tag[lc]] if lc in bank_tag else []) + \
                                ([f"wq{d2}" for d2 in range(NDC)] if ot == 0 and lc == 0 else [])
                    mm(lc, psA[lc][:],
                       wq_sb[:, dc * D + ot * 128: dc * D + ot * 128 + 128],
                       xT[:, dc * T + lc * 512: dc * T + lc * 512 + 512],
                       start=(dc == 0), stop=(dc == NDC - 1), waits=waits,
                       tag=(f"q{ot}l{lc}" if dc == NDC - 1 else None))
            for lc in range(4):
                phi_evac(f"q{ot}l{lc}", psA[lc][:],
                         qf[:, ot * T + lc * 512: ot * T + lc * 512 + 512], lc % 2)
                bank_tag[lc] = f"q{ot}l{lc}_done"

        def wsum_reduce(mi, name):
            c0 = 32 * mi
            rcol = 100 + 64 * mi
            mmw = [f"p1c{16 * mi + 15}", "ones_f", "ones_row"]
            mmw += [f"kvacc{NTT - 1}"]
            S.step("t", lambda t: t.matmul(psB[2][0:1, rcol:rcol + 32], ones_f[:, 0:1],
                                           wsums[:, c0:c0 + 32], start=False, stop=False,
                                           skip_group_check=True),
                   "ts", 1, waits=mmw, tag=f"wr{name}_mm")
            S.step("v", lambda v: v.tensor_reduce(stats[0:1, 160:161],
                                                  psB[2][0:1, rcol:rcol + 32],
                                                  AX.X, ALU.add),
                   "vs", 1, waits=[f"wr{name}_mm"], tag=f"wr{name}_red", fence=True)
            S.step("v", lambda v: v.tensor_scalar(stats[0:1, 161:162], stats[0:1, 160:161],
                                                  1.0 / (D * INNER), EPS_LN, ALU.mult, ALU.max),
                   "vs", 1, tag=f"wr{name}_mc", fence=True)
            S.step("t", lambda t: t.matmul(psB[2][0:128, rcol + 40:rcol + 41], onesrow[:],
                                           stats[0:1, 161:162], start=False, stop=False,
                                           skip_group_check=True),
                   "ts", 1, waits=[f"wr{name}_mc"], tag=f"wr{name}_bc")
            mc = MCLIP[name]
            sw = SW[name]
            S.step("v", lambda v: v.tensor_copy(stats[:, mc:mc + 1],
                                                psB[2][0:128, rcol + 40:rcol + 41]),
                   "vs", 1, waits=[f"wr{name}_bc"], tag=f"wr{name}_m0", fence=True)
            S.step("v", lambda v: v.tensor_copy(stats[:, mc + 1:mc + 2], stats[:, mc:mc + 1]),
                   "vs", 1, tag=f"wr{name}_m1", fence=True)
            S.step("v", lambda v: v.reciprocal(stats[:, sw:sw + 2], stats[:, mc:mc + 2]),
                   "vs", 1, tag=f"wr{name}_sw", fence=True)

        AR.release("xT", "wq_sb")
        wsum_reduce(0, "g")

        nd = AR.alloc("nd", HP * 136, BF16)
        for hp in range(HP):
            pbank = psB[0] if hp < 4 else psB[1]
            off = (hp % 4) * 128
            nb = hp * 136
            S.step("v", lambda v, nb=nb: v.memset(nd[:, nb:nb + 136], 0.0),
                   "vs", 1, waits=[f"kvacc{NTT-1}"] if hp == 0 else (), tag=f"nd{hp}_z")
            S.step("v", lambda v, nb=nb, pbank=pbank, off=off:
                   v.tensor_copy(nd[0:64, nb:nb + 64], pbank[0:64, off:off + 64]),
                   "vs", 1, tag=f"nd{hp}_a")
            S.step("v", lambda v, nb=nb, pbank=pbank, off=off:
                   v.tensor_copy(nd[64:128, nb + 64:nb + 128],
                                 pbank[64:128, off + 64:off + 128]),
                   "vs", 1, tag=f"nd{hp}_b")
            S.step("v", lambda v, nb=nb, hp=hp:
                   v.tensor_copy(nd[0:64, nb + 128:nb + 129], psB[2][0:64, hp:hp + 1]),
                   "vs", 1, tag=f"nd{hp}_c")
            S.step("v", lambda v, nb=nb, hp=hp:
                   v.tensor_copy(nd[64:128, nb + 129:nb + 130], psB[2][64:128, hp:hp + 1]),
                   "vs", 1, tag=f"nd{hp}_done")

        concat_tl = [AR.alloc("cc0", D, BF16), AR.alloc("cc1", D, BF16)]
        attnT_tl = [AR.alloc("at0", D, BF16), AR.alloc("at1", D, BF16)]
        xq_stage = AR.alloc("xq_stage", D, BF16)
        xqT = AR.alloc("xqT", NDC * T, BF16)
        i32st = AR.alloc("i32st", 2048, I32)

        def num_den(tt):
            pb = tt % 2
            for g4 in range(2):
                hps = [g4 * 4 + i for i in range(4)]
                for i, hp in enumerate(hps):
                    w = []
                    if tt == 0:
                        w += [f"nd{hp}_done", f"q{hp}l3_done"]
                    if i == 0:
                        w += [bank_tag[j] for j in range(4) if j in bank_tag]
                    mm(i, psA[i][:, 0:130],
                       qf[:, hp * T + tt * 128: hp * T + tt * 128 + 128],
                       nd[:, hp * 136: hp * 136 + 130],
                       start=True, stop=True, waits=w, tag=f"n{tt}h{hp}")
                for i, hp in enumerate(hps):
                    S.step("v", lambda v, i=i: v.tensor_scalar(rc[:, 4 * i:4 * i + 2],
                                                               psA[i][:, 128:130],
                                                               EPS_ATTN, None, ALU.add),
                           "vs", 1, waits=[f"n{tt}h{hp}"], tag=f"rca{tt}_{hp}")
                for i, hp in enumerate(hps):
                    S.step("v", lambda v, i=i: v.reciprocal(rc[:, 4 * i + 2:4 * i + 4],
                                                            rc[:, 4 * i:4 * i + 2]),
                           "vs", 1, tag=f"rcr{tt}_{hp}", fence=(i == 3))
                for i, hp in enumerate(hps):
                    cb = hp * 128
                    S.step("v", lambda v, i=i, cb=cb, pb=pb:
                           v.tensor_scalar(concat_tl[pb][:, cb:cb + 64], psA[i][:, 0:64],
                                           rc[:, 4 * i + 2:4 * i + 3], None, ALU.mult),
                           "vs", 1)
                    S.step("v", lambda v, i=i, cb=cb, pb=pb:
                           v.tensor_scalar(concat_tl[pb][:, cb + 64:cb + 128],
                                           psA[i][:, 64:128],
                                           rc[:, 4 * i + 3:4 * i + 4], None, ALU.mult),
                           "vs", 1, tag=f"cc{tt}_{hp}")
                    bank_tag[i] = f"cc{tt}_{hp}"

        def transpose_concat(tt):
            pb = tt % 2
            w = [f"cc{tt}_{hp}" for hp in range(HP)] + ["ident"]
            if 7 in bank_tag:
                w.append(bank_tag[7])
            for dq in range(NDC):
                S.step("t", lambda t, dq=dq, pb=pb:
                       t.transpose(psT[:, dq * 128:(dq + 1) * 128],
                                   concat_tl[pb][:, dq * 128:(dq + 1) * 128], sident),
                       *(("ts", 1) if dq == NDC - 1 else (None, 0)),
                       waits=(w if dq == 0 else ()),
                       tag=(f"tc{tt}" if dq == NDC - 1 else None))
            S.step("v", lambda v, pb=pb:
                   v.tensor_copy(attnT_tl[pb], psT[:, 0:1024]),
                   "vs", 1, waits=[f"tc{tt}"], tag=f"tc{tt}_e")
            bank_tag[7] = f"tc{tt}_e"

        def wo_resid_ln1(tt):
            pb = tt % 2
            for dq in range(NDC):
                for b in range(2):
                    w = []
                    if dq == 0:
                        w = [f"tc{tt}_e"] + [f"wo{d2}" for d2 in range(NDC) if tt == 0]
                        if tt == 0:
                            w += [f"nd{HP-1}_done"]
                        w += [bank_tag.get(f"wo{b}_bank")] if f"wo{b}_bank" in bank_tag else []
                    mm(None, psB[b][:],
                       attnT_tl[pb][:, dq * 128:(dq + 1) * 128],
                       wo_sb[:, dq * D + b * 512: dq * D + b * 512 + 512],
                       start=(dq == 0), stop=(dq == NDC - 1),
                       waits=[x for x in w if x],
                       tag=(f"wo{tt}b{b}" if dq == NDC - 1 else None))
            for b in range(2):
                S.step("v", lambda v, b=b, tt=tt:
                       v.scalar_tensor_tensor(xtok_tl[tt % 2][:, b * 512:(b + 1) * 512],
                                              psB[b][:], 1.0,
                                              xtok_tl[tt % 2][:, b * 512:(b + 1) * 512],
                                              ALU.mult, ALU.add,
                                              accum_out=stats[:, 2 * tt + b:2 * tt + b + 1]),
                       "vs", 1, waits=[f"wo{tt}b{b}", f"xtok{tt}"], tag=f"r1{tt}b{b}")
                bank_tag[f"wo{b}_bank"] = f"r1{tt}b{b}"
                S.step("s", lambda s, b=b, tt=tt:
                       s.activation(tmp[2], xtok_tl[tt % 2][:, b * 512:(b + 1) * 512],
                                    AF.Square,
                                    accum_out=stats[:, 32 + 2 * tt + b:32 + 2 * tt + b + 1]),
                       "ss", 1, waits=[f"r1{tt}b{b}"], tag=f"sq{tt}b{b}")
            c_sum, c_sq = 2 * tt, 32 + 2 * tt
            c_m, c_v, c_nm = 64 + tt, 80 + tt, 96 + tt
            S.step("v", lambda v: v.tensor_tensor(stats[:, c_m:c_m + 1],
                                                  stats[:, c_sum:c_sum + 1],
                                                  stats[:, c_sum + 1:c_sum + 2], ALU.add),
                   "vs", 1, waits=[f"r1{tt}b0", f"r1{tt}b1"], tag=f"ln1a{tt}", fence=True)
            S.step("v", lambda v: v.tensor_scalar(stats[:, c_m:c_m + 1],
                                                  stats[:, c_m:c_m + 1], 1.0 / D, None,
                                                  ALU.mult),
                   "vs", 1, tag=f"ln1b{tt}", fence=True)
            S.step("v", lambda v: v.tensor_tensor(stats[:, c_v:c_v + 1],
                                                  stats[:, c_sq:c_sq + 1],
                                                  stats[:, c_sq + 1:c_sq + 2], ALU.add),
                   "vs", 1, waits=[f"sq{tt}b0", f"sq{tt}b1"], tag=f"ln1c{tt}", fence=True)
            S.step("v", lambda v: v.tensor_scalar(stats[:, c_v:c_v + 1],
                                                  stats[:, c_v:c_v + 1], 1.0 / D, None,
                                                  ALU.mult),
                   "vs", 1, tag=f"ln1d{tt}", fence=True)
            S.step("v", lambda v: v.tensor_tensor(stats[:, 112:113], stats[:, c_m:c_m + 1],
                                                  stats[:, c_m:c_m + 1], ALU.mult),
                   "vs", 1, tag=f"ln1e{tt}", fence=True)
            S.step("v", lambda v: v.tensor_tensor(stats[:, c_v:c_v + 1],
                                                  stats[:, c_v:c_v + 1],
                                                  stats[:, 112:113], ALU.subtract),
                   "vs", 1, tag=f"ln1f{tt}", fence=True)
            S.step("s", lambda s: s.activation(stats[:, c_v:c_v + 1], stats[:, c_v:c_v + 1],
                                               AF.Ln, bias=EPS_LN),
                   "ss", 1, waits=[f"ln1f{tt}"], tag=f"ln1g{tt}", fence=True)
            S.step("s", lambda s: s.activation(stats[:, c_v:c_v + 1], stats[:, c_v:c_v + 1],
                                               AF.Exp, scale=-0.5),
                   "ss", 1, tag=f"ln1h{tt}", fence=True)
            S.step("v", lambda v: v.scalar_tensor_tensor(stats[:, c_nm:c_nm + 1],
                                                         stats[:, c_m:c_m + 1], -1.0,
                                                         stats[:, c_v:c_v + 1],
                                                         ALU.mult, ALU.mult),
                   "vs", 1, waits=[f"ln1h{tt}"], tag=f"ln1i{tt}", fence=True)
            S.step("v", lambda v, tt=tt:
                   v.tensor_scalar(xtok_tl[tt % 2], xtok_tl[tt % 2],
                                   stats[:, c_v:c_v + 1], stats[:, c_nm:c_nm + 1],
                                   ALU.mult, ALU.add),
                   "vs", 1, tag=f"y1{tt}")
            S.step("g", lambda g, tt=tt: g.dma_start(y1_d[tt * 128:(tt + 1) * 128, :],
                                                     xtok_tl[tt % 2]),
                   "ld", 16, waits=[f"y1{tt}"], tag=f"y1sp{tt}")
            c_mx, c_sx, c_rx = 112, 128 + tt, 144 + tt
            S.step("v", lambda v: v.tensor_reduce(stats[:, c_mx:c_mx + 1], xtok_tl[tt % 2],
                                                  AX.X, ALU.max, apply_absolute_value=True),
                   "vs", 1, waits=[f"y1{tt}"], tag=f"xqa{tt}", fence=True)
            S.step("v", lambda v: v.tensor_scalar(stats[:, c_mx:c_mx + 1],
                                                  stats[:, c_mx:c_mx + 1], EPS_LN, None,
                                                  ALU.max),
                   "vs", 1, tag=f"xqb{tt}", fence=True)
            S.step("v", lambda v: v.tensor_copy(stats[:, c_mx + 1:c_mx + 2],
                                                stats[:, c_mx:c_mx + 1]),
                   "vs", 1, tag=f"xqc{tt}", fence=True)
            S.step("v", lambda v: v.reciprocal(stats[:, 114:116], stats[:, c_mx:c_mx + 2]),
                   "vs", 1, tag=f"xqd{tt}", fence=True)
            S.step("v", lambda v: v.tensor_scalar(stats[:, c_sx:c_sx + 1], stats[:, 114:115],
                                                  127.0, None, ALU.mult),
                   "vs", 1, tag=f"xqe{tt}", fence=True)
            S.step("v", lambda v: v.tensor_copy(stats[:, 114:115], stats[:, c_sx:c_sx + 1]),
                   "vs", 1, tag=f"xqf{tt}", fence=True)
            S.step("v", lambda v: v.reciprocal(stats[:, 116:118], stats[:, 114:116]),
                   "vs", 1, tag=f"xqg{tt}", fence=True)
            S.step("v", lambda v: v.tensor_copy(stats[:, c_rx:c_rx + 1], stats[:, 116:117]),
                   "vs", 1, tag=f"xqh{tt}", fence=True)
            S.step("v", lambda v, tt=tt: v.tensor_scalar(i32st[:, 0:1024], xtok_tl[tt % 2],
                                                         stats[:, c_sx:c_sx + 1], None,
                                                         ALU.mult),
                   "vs", 1, tag=f"xqi{tt}")
            S.step("v", lambda v: v.tensor_scalar(xq_stage, i32st[:, 0:1024],
                                                  127.0, -128.0, ALU.min, ALU.max),
                   "vs", 1, waits=([f"tx{tt - 1}"] if tt >= 1 else []), tag=f"xqj{tt}")

        def transpose_xq(tt):
            w = [f"xqj{tt}", bank_tag[7]]
            for dc in range(NDC):
                S.step("t", lambda t, dc=dc:
                       t.transpose(psT[:, dc * 128:(dc + 1) * 128],
                                   xq_stage[:, dc * 128:(dc + 1) * 128], sident),
                       *(("ts", 1) if dc == NDC - 1 else (None, 0)),
                       waits=(w if dc == 0 else ()),
                       tag=(f"tx{tt}" if dc == NDC - 1 else None))
            S.step("v", lambda v, tt=tt:
                   v.tensor_copy(xqT[:].rearrange("p (dc t) -> p dc t", dc=NDC)
                                 [:, :, tt * 128:(tt + 1) * 128],
                                 psT[:, 0:1024].rearrange("p (dc c) -> p dc c", dc=NDC)),
                   "vs", 1, waits=[f"tx{tt}"], tag=f"tx{tt}_e")
            bank_tag[7] = f"tx{tt}_e"

        def xtok_load(tt):
            w = [f"y1sp{tt - 2}"] if tt >= 2 else []
            S.step("y", lambda y, tt=tt:
                   y.dma_start(xtok_tl[tt % 2], x_tok[tt * 128:(tt + 1) * 128, :]),
                   "ls", 16, waits=w, tag=f"xtok{tt}")

        xtok_load(0)
        xtok_load(1)
        for tt in range(NTT):
            num_den(tt)
            transpose_concat(tt)
            if tt >= 1:
                transpose_xq(tt - 1)
            wo_resid_ln1(tt)
            if tt + 2 < NTT:
                xtok_load(tt + 2)
        transpose_xq(NTT - 1)

        wsum_reduce(1, "u")
        wsum_reduce(2, "d")
        bank_tag[8] = "wrd_sw"
        AR.release("qf", "nd", "wo_sb", "dump", "cc0", "cc1", "at0", "at1",
                   "xq_stage", "kf0", "kf1", "v0", "v1", "xtok0", "xtok1",
                   "tmp3")

        tern_g = [AR.alloc(f"tern_g{i}", 4 * INNER, FP8) for i in range(2)]
        tern_u = [AR.alloc(f"tern_u{i}", 4 * INNER, FP8) for i in range(2)]
        tern_d = [AR.alloc(f"tern_d{i}", 16 * D, FP8) for i in range(2)]
        TERN = {"g": tern_g, "u": tern_u}

        def tern_gu_sl(mat, dc, off, n):
            seg = TERN[mat][dc // 4]
            base = (dc % 4) * INNER + off
            return seg[:, base:base + n]

        def tern_d_sl(it, off, n):
            seg = tern_d[it // 16]
            base = (it % 16) * D + off
            return seg[:, base:base + n]

        def p2_consume(si, ent):
            mat = ent[0]
            buf = si % 2
            S.step("v", lambda v, buf=buf, mat=mat:
                   v.tensor_scalar(i32st, wstage[buf], stats[:, SW[mat]:SW[mat] + 1],
                                   None, ALU.mult),
                   "vs", 1, waits=[f"p2L{si}", f"wr{mat}_sw"], tag=f"p2q{si}a")
            if mat in ("g", "u"):
                _, ti, h = ent
                dst = tern_gu_sl(mat, ti, h * 2048, 2048)
            else:
                _, sd, _ = ent
                dst = tern_d_sl(2 * sd, 0, 2048)
            S.step("v", lambda v, dst=dst:
                   v.tensor_scalar(dst, i32st, 1.0, -1.0, ALU.min, ALU.max),
                   "vs", 1, tag=f"p2q{si}b")

        for si, ent in enumerate(stage_list):
            if si >= 2:
                p2_consume(si - 2, stage_list[si - 2])
            S.step("y", stage_load_fn(si % 2, ent), "ls", 16,
                   waits=(["p1c47"] if si < 2 else [f"p2q{si - 2}b"]), tag=f"p2L{si}")
        p2_consume(46, stage_list[46])
        p2_consume(47, stage_list[47])

        y1c_tl = [AR.alloc("y1c0", D, F32), AR.alloc("y1c1", D, F32)]
        gateh_seg = [AR.alloc(f"gateh{i}", 2048, F32) for i in range(2)]
        hq = AR.alloc("hq", INNER, BF16)
        hqT_seg = [AR.alloc("hqT0", INNER // 2, BF16), AR.alloc("hqT1", INNER // 2, BF16)]

        def hqT_sl(col, n):
            seg = col // 2048
            return hqT_seg[seg][:, col - seg * 2048: col - seg * 2048 + n]
        y2_seg = [AR.alloc("y2a", D // 2, F32), AR.alloc("y2b", D // 2, F32)]

        def gateh_sl(oc, n=512):
            return gateh_seg[oc // 4][:, (oc % 4) * 512:(oc % 4) * 512 + n]

        def ffn_front(c):
            cb = 0
            ycb = c % 2
            S.step("y", lambda y, c=c, ycb=ycb: y.dma_start(y1c_tl[ycb],
                                                            y1_d[c * 128:(c + 1) * 128, :]),
                   "ls", 16,
                   waits=[f"y1sp{c}"] + ([f"r2{c - 2}b1"] if c >= 2 else []),
                   tag=f"y1r{c}")
            cgc = 204 + 2 * (c % 2)
            S.step("v", lambda v, c=c, cgc=cgc: v.tensor_tensor(stats[:, cgc:cgc + 1],
                                                       stats[:, 144 + c:145 + c],
                                                       stats[:, MCLIP['g']:MCLIP['g'] + 1],
                                                       ALU.mult),
                   "vs", 1,
                   waits=[f"wrg_sw", f"xqh{c}"] +
                         ([f"ge{c - 2}w1i3"] if c >= 2 else []),
                   tag=f"cg{c}", fence=True)
            S.step("v", lambda v, c=c: v.tensor_tensor(stats[:, 185:186],
                                                       stats[:, 144 + c:145 + c],
                                                       stats[:, MCLIP['u']:MCLIP['u'] + 1],
                                                       ALU.mult),
                   "vs", 1, waits=[f"wru_sw"], tag=f"cu{c}", fence=True)

            for wave in range(2):
                for dc in range(NDC):
                    for i in range(4):
                        oc = wave * 4 + i
                        w = []
                        if dc == 0:
                            w = [bank_tag[i]] if i in bank_tag else []
                            if c == 0 and wave == 0:
                                w += [f"tx{NTT-1}_e"]
                            if i == 0:
                                w += [f"p2q{2 * t2 + (1 if oc >= 4 else 0)}b"
                                      for t2 in range(NDC)] if c == 0 else []
                        mm(i, psA[i][:],
                           xqT[:, dc * T + c * 128: dc * T + c * 128 + 128],
                           tern_gu_sl("g", dc, oc * 512, 512),
                           start=(dc == 0), stop=(dc == NDC - 1),
                           waits=w, tag=(f"g{c}w{wave}i{i}" if dc == NDC - 1 else None))
                for i in range(4):
                    oc = wave * 4 + i
                    S.step("s", lambda s, i=i, oc=oc, c=c, cgc=cgc:
                           s.activation(gateh_sl(oc), psA[i][:],
                                        AF.Silu, scale=stats[:, cgc:cgc + 1]),
                           "ss", 1,
                           waits=[f"g{c}w{wave}i{i}", f"cg{c}"] +
                                 ([f"hqa{c - 1}_1"] if (c >= 1 and wave == 0 and i == 0)
                                  else []),
                           tag=f"ge{c}w{wave}i{i}")
                    bank_tag[i] = f"ge{c}w{wave}i{i}"
            for wave in range(2):
                for dc in range(NDC):
                    for i in range(4):
                        oc = wave * 4 + i
                        w = []
                        if dc == 0:
                            w = [bank_tag[i]] if i in bank_tag else []
                            if i == 0 and c == 0:
                                w += [f"p2q{16 + 2 * t2 + (1 if oc >= 4 else 0)}b"
                                      for t2 in range(NDC)]
                        mm(i, psA[i][:],
                           xqT[:, dc * T + c * 128: dc * T + c * 128 + 128],
                           tern_gu_sl("u", dc, oc * 512, 512),
                           start=(dc == 0), stop=(dc == NDC - 1),
                           waits=w, tag=(f"u{c}w{wave}i{i}" if dc == NDC - 1 else None))
                for i in range(4):
                    oc = wave * 4 + i
                    S.step("v", lambda v, i=i, oc=oc, c=c:
                           v.scalar_tensor_tensor(gateh_sl(oc),
                                                  psA[i][:], stats[:, 185:186],
                                                  gateh_sl(oc),
                                                  ALU.mult, ALU.mult),
                           "vs", 1,
                           waits=[f"u{c}w{wave}i{i}", f"ge{c}w{wave}i{i}", f"cu{c}"],
                           tag=f"h{c}w{wave}i{i}")
                    bank_tag[i] = f"h{c}w{wave}i{i}"

            hw = [f"h{c}w{wv}i{i}" for wv in range(2) for i in range(4)]
            S.step("v", lambda v: v.tensor_reduce(stats[:, 186:187], gateh_seg[0],
                                                  AX.X, ALU.max, apply_absolute_value=True),
                   "vs", 1, waits=hw, tag=f"hm0{c}", fence=True)
            S.step("v", lambda v: v.tensor_reduce(stats[:, 187:188], gateh_seg[1],
                                                  AX.X, ALU.max, apply_absolute_value=True),
                   "vs", 1, tag=f"hm1{c}", fence=True)
            S.step("v", lambda v: v.tensor_tensor(stats[:, 188:189], stats[:, 186:187],
                                                  stats[:, 187:188], ALU.max),
                   "vs", 1, tag=f"hm2{c}", fence=True)
            S.step("v", lambda v: v.tensor_scalar(stats[:, 188:189], stats[:, 188:189],
                                                  EPS_LN, None, ALU.max),
                   "vs", 1, tag=f"hm3{c}", fence=True)
            S.step("v", lambda v: v.tensor_copy(stats[:, 189:190], stats[:, 188:189]),
                   "vs", 1, tag=f"hm4{c}", fence=True)
            S.step("v", lambda v: v.reciprocal(stats[:, 190:192], stats[:, 188:190]),
                   "vs", 1, tag=f"hm5{c}", fence=True)
            S.step("v", lambda v: v.tensor_scalar(stats[:, 192:193], stats[:, 190:191],
                                                  127.0, None, ALU.mult),
                   "vs", 1, tag=f"hm6{c}", fence=True)
            S.step("v", lambda v: v.tensor_copy(stats[:, 190:191], stats[:, 192:193]),
                   "vs", 1, tag=f"hm7{c}", fence=True)
            S.step("v", lambda v: v.reciprocal(stats[:, 193:195], stats[:, 190:192]),
                   "vs", 1, tag=f"hm8{c}", fence=True)
            cdc = 208 + (c % 2)
            S.step("v", lambda v, cdc=cdc: v.tensor_tensor(stats[:, cdc:cdc + 1],
                                                           stats[:, 193:194],
                                                           stats[:, MCLIP['d']:MCLIP['d'] + 1],
                                                           ALU.mult),
                   "vs", 1, waits=[f"wrd_sw"], tag=f"cd{c}", fence=True)
            for hh in range(2):
                S.step("v", lambda v, hh=hh:
                       v.tensor_scalar(i32st, gateh_seg[hh],
                                       stats[:, 192:193], None, ALU.mult),
                       "vs", 1, tag=f"hqa{c}_{hh}")
                S.step("v", lambda v, hh=hh:
                       v.tensor_scalar(hq[:, hh * 2048:(hh + 1) * 2048], i32st,
                                       127.0, -128.0, ALU.min, ALU.max),
                       "vs", 1, tag=f"hqb{c}_{hh}")

        psB2bf = psB[2][:].bitcast(BF16)

        def ffn_back_T(c):
            for bt in range(4):
                use_t = (bt % 2 == 0)
                bk = 7 if use_t else 8
                bank = psT if use_t else psB2bf
                w = [f"hqb{c}_{bt // 2}"] + ([bank_tag[bk]] if bk in bank_tag else [])
                for j in range(8):
                    it = bt * 8 + j
                    S.step("t", lambda t, it=it, j=j, bank=bank:
                           t.transpose(bank[:, j * 128:(j + 1) * 128],
                                       hq[:, it * 128:(it + 1) * 128], sident),
                           *(("ts", 1) if j == 7 else (None, 0)),
                           waits=(w if j == 0 else ()),
                           tag=(f"th{c}_{bt}" if j == 7 else None))
                S.step("v", lambda v, bank=bank, bt=bt:
                       v.tensor_copy(hqT_sl(bt * 1024, 1024), bank[:, 0:1024]),
                       "vs", 1, waits=[f"th{c}_{bt}"], tag=f"th{c}_{bt}_e")
                bank_tag[bk] = f"th{c}_{bt}_e"

        def ffn_back_D(c):
            cb = 0
            ycb = c % 2
            for it in range(32):
                for b in range(2):
                    w = []
                    if it == 0:
                        w = [f"th{c}_0_e"]
                        w += [f"r2{c - 1}b{b}"] if c >= 1 else []
                        if c == 0:
                            w += [f"p2q{32 + sd}b" for sd in range(16)]
                            w += [f"r1{NTT-1}b{b}"]
                    elif it % 8 == 0:
                        w = [f"th{c}_{it // 8}_e"]
                    mm(None, psB[b][:],
                       hqT_sl(it * 128, 128),
                       tern_d_sl(it, b * 512, 512),
                       start=(it == 0), stop=(it == 31),
                       waits=w, tag=(f"d{c}b{b}" if it == 31 else None))
            for b in range(2):
                cdc = 208 + (c % 2)
                S.step("v", lambda v, b=b, c=c, ycb=ycb, cdc=cdc:
                       v.scalar_tensor_tensor(y2_seg[b][:],
                                              psB[b][:], stats[:, cdc:cdc + 1],
                                              y1c_tl[ycb][:, b * 512:(b + 1) * 512],
                                              ALU.mult, ALU.add,
                                              accum_out=stats[:, 196 + b:197 + b]),
                       "vs", 1,
                       waits=[f"d{c}b{b}", f"y1r{c}", f"cd{c}"] +
                             ([f"yst{c - 1}s{b}"] if c >= 1 else []),
                       tag=f"r2{c}b{b}")
                S.step("s", lambda s, b=b, c=c:
                       s.activation(tmp[2], y2_seg[b][:],
                                    AF.Square, accum_out=stats[:, 198 + b:199 + b]),
                       "ss", 1, waits=[f"r2{c}b{b}"], tag=f"sq2{c}b{b}")
            S.step("v", lambda v: v.tensor_tensor(stats[:, 200:201], stats[:, 196:197],
                                                  stats[:, 197:198], ALU.add),
                   "vs", 1, waits=[f"r2{c}b0", f"r2{c}b1"], tag=f"l2a{c}", fence=True)
            S.step("v", lambda v: v.tensor_scalar(stats[:, 200:201], stats[:, 200:201],
                                                  1.0 / D, None, ALU.mult),
                   "vs", 1, tag=f"l2b{c}", fence=True)
            S.step("v", lambda v: v.tensor_tensor(stats[:, 201:202], stats[:, 198:199],
                                                  stats[:, 199:200], ALU.add),
                   "vs", 1, waits=[f"sq2{c}b0", f"sq2{c}b1"], tag=f"l2c{c}", fence=True)
            S.step("v", lambda v: v.tensor_scalar(stats[:, 201:202], stats[:, 201:202],
                                                  1.0 / D, None, ALU.mult),
                   "vs", 1, tag=f"l2d{c}", fence=True)
            S.step("v", lambda v: v.tensor_tensor(stats[:, 202:203], stats[:, 200:201],
                                                  stats[:, 200:201], ALU.mult),
                   "vs", 1, tag=f"l2e{c}", fence=True)
            S.step("v", lambda v: v.tensor_tensor(stats[:, 201:202], stats[:, 201:202],
                                                  stats[:, 202:203], ALU.subtract),
                   "vs", 1, tag=f"l2f{c}", fence=True)
            S.step("s", lambda s: s.activation(stats[:, 201:202], stats[:, 201:202],
                                               AF.Ln, bias=EPS_LN),
                   "ss", 1, waits=[f"l2f{c}"], tag=f"l2g{c}", fence=True)
            S.step("s", lambda s: s.activation(stats[:, 201:202], stats[:, 201:202],
                                               AF.Exp, scale=-0.5),
                   "ss", 1, tag=f"l2h{c}", fence=True)
            S.step("v", lambda v: v.scalar_tensor_tensor(stats[:, 203:204],
                                                         stats[:, 200:201], -1.0,
                                                         stats[:, 201:202],
                                                         ALU.mult, ALU.mult),
                   "vs", 1, waits=[f"l2h{c}"], tag=f"l2i{c}", fence=True)
            for b in range(2):
                S.step("v", lambda v, b=b: v.tensor_scalar(y2_seg[b][:], y2_seg[b][:],
                                                           stats[:, 201:202],
                                                           stats[:, 203:204],
                                                           ALU.mult, ALU.add),
                       "vs", 1, tag=f"y2{c}s{b}")
                S.step("g", lambda g, c=c, b=b:
                       g.dma_start(y_d[c * 128:(c + 1) * 128, b * 512:(b + 1) * 512],
                                   y2_seg[b][:]),
                       "ld", 16, waits=[f"y2{c}s{b}"], tag=f"yst{c}s{b}")

        ffn_front(0)
        for c in range(1, NTT):
            ffn_back_T(c - 1)
            ffn_front(c)
            ffn_back_D(c - 1)
        ffn_back_T(NTT - 1)
        ffn_back_D(NTT - 1)

        tot_ld = S.totals.get("ld", 0)
        tot_ls = S.totals.get("ls", 0)

        S.check_deadlock()
        block = ctx.enter_context(nc.Block())

        @block.tensor
        def _(t):
            S.emit("t", t, sems)

        @block.vector
        def _(v):
            S.emit("v", v, sems)

        @block.scalar
        def _(s):
            S.emit("s", s, sems)

        @block.gpsimd
        def _(g):
            S.emit("g", g, sems)

        @block.sync
        def _(sy):
            S.emit("y", sy, sems)
            sy.wait_ge(sems["ld"], tot_ld)
            sy.wait_ge(sems["ls"], tot_ls)

    return nc


_NC_CACHE = None


def kernel(L0_attn_x, L0_attn_Wq, L0_attn_Wk, L0_attn_Wv, L0_attn_Wo,
           L0_ffn_w_gate, L0_ffn_w_up, L0_ffn_w_down):
    global _NC_CACHE
    x = np.ascontiguousarray(np.asarray(L0_attn_x, dtype=np.float32))
    wq = np.ascontiguousarray(np.asarray(L0_attn_Wq, dtype=np.float32))
    wk = np.ascontiguousarray(np.asarray(L0_attn_Wk, dtype=np.float32))
    wv = np.ascontiguousarray(np.asarray(L0_attn_Wv, dtype=np.float32))
    wo = np.ascontiguousarray(np.asarray(L0_attn_Wo, dtype=np.float32))
    wtg = np.ascontiguousarray(np.asarray(L0_ffn_w_gate, dtype=np.float32).T)
    wtu = np.ascontiguousarray(np.asarray(L0_ffn_w_up, dtype=np.float32).T)
    wtd = np.ascontiguousarray(np.asarray(L0_ffn_w_down, dtype=np.float32).T)
    ident = np.eye(128, dtype=np.float32)

    if _NC_CACHE is None:
        _NC_CACHE = build_nc()
    nc = _NC_CACHE

    in_maps = []
    for b in range(8):
        xb = np.ascontiguousarray(x[b])
        in_maps.append(dict(x_tok=xb, x_feat=np.ascontiguousarray(xb.T),
                            wq=wq, wk=wk, wv=wv, wo=wo,
                            wtg=wtg, wtu=wtu, wtd=wtd, ident=ident))
    res = run_bass_kernel_spmd(nc, in_maps, core_ids=list(range(8)))
    out = np.stack([np.asarray(res.results[b]["y"], dtype=np.float32) for b in range(8)])
    return out
